# version 46
# baseline (speedup 1.0000x reference)
import numpy as np
import ml_dtypes
from contextlib import ExitStack

import concourse.bass as bass
import concourse.mybir as mybir
from concourse.bass_utils import run_bass_kernel_spmd

F32 = mybir.dt.float32
BF16 = mybir.dt.bfloat16
AF = mybir.ActivationFunctionType
ALU = mybir.AluOpType
AX = mybir.AxisListType
bf16 = ml_dtypes.bfloat16

NCORE = 8
B, T, D = 32, 2048, 1024
NH, DH, WIN = 16, 64, 128
NT = T // 128
BPC = B // NCORE
E, R, ER = 8, 64, 512
NEG = -1.0e30
EPS = 1e-5


class Sched:

    def __init__(self):
        self.ops = []
        self.buf_w = {}
        self.buf_r = {}

    def emit(self, engine, fn, reads=(), writes=(), sem=None, dma=False):
        deps = set()
        for k in reads:
            w = self.buf_w.get(k)
            if w is not None:
                deps.add(w)
        for k in writes:
            w = self.buf_w.get(k)
            if w is not None:
                deps.add(w)
            for rid in self.buf_r.get(k, {}).values():
                deps.add(rid)
        oid = len(self.ops)
        self.ops.append(dict(engine=engine, fn=fn, sem=sem or engine,
                             deps=deps, dma=dma))
        for k in reads:
            self.buf_r.setdefault(k, {})[engine] = oid
        for k in writes:
            self.buf_w[k] = oid
            self.buf_r[k] = {}
        return oid

    def force_writer(self, keys, oid):
        for k in keys:
            self.buf_w[k] = oid
            self.buf_r[k] = {}

    def materialize(self, block, sems, engines):
        def keep(p, c):
            if (p["engine"] == c["engine"] == "tensor"
                    and not p["dma"] and not c["dma"]):
                return False
            return True

        needed = set()
        for op in self.ops:
            for d in op["deps"]:
                if keep(self.ops[d], op):
                    needed.add(d)
        counts = {}
        for oid, op in enumerate(self.ops):
            op["inc"] = op["dma"] or (oid in needed)
            if op["inc"]:
                q = 16 if op["dma"] else 1
                counts[op["sem"]] = counts.get(op["sem"], 0) + q
                op["val"] = counts[op["sem"]]
        for op in self.ops:
            waits = {}
            for d in op["deps"]:
                p = self.ops[d]
                if not keep(p, op):
                    continue
                waits[p["sem"]] = max(waits.get(p["sem"], 0), p["val"])
            op["waits"] = waits
        self.totals = counts

        def make_body(eng_name, final_waits=()):
            def body(eng):
                last = {}
                for op in self.ops:
                    if op["engine"] != eng_name:
                        continue
                    for snm, val in sorted(op["waits"].items()):
                        if val > last.get(snm, 0):
                            eng.wait_ge(sems[snm], val)
                            last[snm] = val
                    inst = op["fn"](eng)
                    if op["inc"]:
                        inst.then_inc(sems[op["sem"]], 16 if op["dma"] else 1)
                for snm, val in final_waits:
                    if val > last.get(snm, 0):
                        eng.wait_ge(sems[snm], val)
            return body
        return make_body


def _host_consts():
    inv = (np.float32(1.0) /
           np.float32(10000.0) ** (np.arange(0, DH, 2, dtype=np.float32)
                                   / np.float32(DH)))
    freqs = np.arange(T, dtype=np.float32)[:, None] * inv[None, :]
    emb = np.concatenate([freqs, freqs], axis=-1)
    cos = np.cos(emb).astype(np.float32)
    sin = np.sin(emb).astype(np.float32)
    sgn = np.concatenate([-np.ones(DH // 2, np.float32),
                          np.ones(DH // 2, np.float32)])
    ssin = sin * sgn[None, :]

    def tabify(t):
        return np.ascontiguousarray(
            t.reshape(NT, 128, DH).transpose(1, 0, 2).reshape(128, NT * DH))

    sm = np.float32(1.0 / np.sqrt(DH))
    r = np.arange(128)[:, None]
    c = np.arange(128)[None, :]
    maskA = np.tile(np.where(r > c, 0.0, NEG), (1, 4)).astype(bf16)
    maskB = np.tile(np.where(r <= c, 0.0, NEG), (1, 4)).astype(bf16)
    return {
        "cq": tabify(cos * sm).astype(bf16), "sq": tabify(ssin * sm).astype(bf16),
        "ck": tabify(cos).astype(bf16), "sk": tabify(ssin).astype(bf16),
        "idn": np.eye(128, dtype=np.float32).astype(bf16),
        "i4f": np.eye(4, dtype=np.float32),
        "i8f": np.eye(8, dtype=np.float32),
        "maskA": maskA, "maskB": maskB,
    }


_LAST_S = None


def build_program(n_b, n_t, max_ops=None):
    nc = bass.Bass("TRN2", target_bir_lowering=False)
    xq = nc.dram_tensor("xq", [n_b, T, D], F32, kind="ExternalInput").ap()
    wqh = nc.dram_tensor("wqh", [D, D], F32, kind="ExternalInput").ap()
    wkh = nc.dram_tensor("wkh", [D, D], F32, kind="ExternalInput").ap()
    wvh = nc.dram_tensor("wvh", [D, D], F32, kind="ExternalInput").ap()
    woh = nc.dram_tensor("woh", [D, D], F32, kind="ExternalInput").ap()
    boh = nc.dram_tensor("boh", [1, D], F32, kind="ExternalInput").ap()
    wrh = nc.dram_tensor("wrh", [D, E], F32, kind="ExternalInput").ap()
    aah = nc.dram_tensor("aah", [D, ER], F32, kind="ExternalInput").ap()
    bwh = nc.dram_tensor("bwh", [ER, D], F32, kind="ExternalInput").ap()
    cqh = nc.dram_tensor("cqh", [128, NT * DH], BF16, kind="ExternalInput").ap()
    sqh = nc.dram_tensor("sqh", [128, NT * DH], BF16, kind="ExternalInput").ap()
    ckh = nc.dram_tensor("ckh", [128, NT * DH], BF16, kind="ExternalInput").ap()
    skh = nc.dram_tensor("skh", [128, NT * DH], BF16, kind="ExternalInput").ap()
    idnh = nc.dram_tensor("idnh", [128, 128], BF16, kind="ExternalInput").ap()
    mAh = nc.dram_tensor("mAh", [128, 512], BF16, kind="ExternalInput").ap()
    mBh = nc.dram_tensor("mBh", [128, 512], BF16, kind="ExternalInput").ap()
    i4fh = nc.dram_tensor("i4fh", [4, 4], F32, kind="ExternalInput").ap()
    i8fh = nc.dram_tensor("i8fh", [8, 8], F32, kind="ExternalInput").ap()
    outq = nc.dram_tensor("outq", [n_b, T, D], F32, kind="ExternalOutput").ap()

    S = Sched()
    stack = ExitStack()
    with stack:
        def sb(name, shape, dt):
            return stack.enter_context(nc.sbuf_tensor(name, shape, dt))

        def ps(name, shape, dt):
            return stack.enter_context(nc.psum_tensor(name, shape, dt))

        wq_sb = sb("wq_sb", [128, 8 * D], BF16)
        wk_sb = sb("wk_sb", [128, 8 * D], BF16)
        wv_sb = sb("wv_sb", [128, 8 * D], BF16)
        wo_sb = sb("wo_sb", [128, 8 * D], BF16)
        aa_sb = sb("aa_sb", [128, 8 * ER], BF16)
        bw_sb = sb("bw_sb", [128, 4 * D], BF16)
        wr_sb = sb("wr_sb", [128, 8 * E], F32)
        bo_sb = sb("bo_sb", [1, D], BF16)
        cq_sb = sb("cq_sb", [128, NT * DH], BF16)
        sq_sb = sb("sq_sb", [128, NT * DH], BF16)
        ck_sb = sb("ck_sb", [128, NT * DH], BF16)
        sk_sb = sb("sk_sb", [128, NT * DH], BF16)
        idn_sb = sb("idn_sb", [128, 128], BF16)
        mA_sb = sb("mA_sb", [128, 512], BF16)
        mB_sb = sb("mB_sb", [128, 512], BF16)
        i4f_sb = sb("i4f_sb", [4, 4], F32)
        i8f_sb = sb("i8f_sb", [8, 8], F32)
        row4a_sb = sb("row4a_sb", [4, 1024], F32)
        row4b_sb = sb("row4b_sb", [4, 1024], F32)
        tmp32 = [sb(f"tmp32_{i}", [128, 1024], F32) for i in range(2)]
        xn0T_sb2 = sb("xn0T_sb2", [128, 32], F32)
        v0c_sb = sb("v0c_sb", [128, 32], F32)
        xn20T_sb = sb("xn20T_sb", [128, 32], F32)
        growT_sb = sb("growT_sb", [8, 4], F32)
        growb_sb = sb("growb_sb", [1, 8], F32)
        st6g_sb = sb("st6g_sb", [4, 12], F32)
        mvg_sb = sb("mvg_sb", [4, 2], F32)
        lntg_sb = sb("lntg_sb", [4, 1], F32)
        rstdg_sb = sb("rstdg_sb", [4, 1], F32)
        onesb = sb("onesb", [128, 128], BF16)
        onesf = sb("onesf", [1, 128], F32)
        eps_sb = sb("eps_sb", [128, 1], F32)
        nh_sb = sb("nh_sb", [128, 1], F32)
        n1_sb = sb("n1_sb", [128, 1], F32)
        x_sb = [sb(f"x_sb{i}", [128, D], F32) for i in range(4)]
        st6 = [sb(f"st6_{i}", [128, 12], F32) for i in range(2)]
        mv = [sb(f"mv_{i}", [128, 2], F32) for i in range(2)]
        lnt = [sb(f"lnt_{i}", [128, 1], F32) for i in range(2)]
        rstd = [sb(f"rstd_{i}", [128, 1], F32) for i in range(2)]
        st6L = sb("st6L", [128, 12], F32)
        mvL = sb("mvL", [128, 2], F32)
        lntL = sb("lntL", [128, 1], F32)
        rstdL = sb("rstdL", [128, 1], F32)
        xn_sb = sb("xn_sb", [128, D], BF16)
        xnT_sb = sb("xnT_sb", [128, D], BF16)
        a_sb = sb("a_sb", [128, D], BF16)
        b_sb = sb("b_sb", [128, D], BF16)
        qsb_sb = sb("qsb_sb", [128, D], BF16)
        ksb_sb = sb("ksb_sb", [128, D], BF16)
        qrot_sb = sb("qrot_sb", [128, D], BF16)
        krot_sb = sb("krot_sb", [128, D], BF16)
        qT_sb = sb("qT_sb", [64, 2 * D], BF16)
        lnz_sb = sb("lnz_sb", [1, 1024], F32)
        r2_sb = [sb(f"r2_sb{i}", [1, 1024], BF16) for i in range(2)]
        kT_sb = [sb(f"kT_sb{i}", [64, 2 * D], BF16) for i in range(2)]
        v_sb = [sb(f"v_sb{i}", [128, D], BF16) for i in range(2)]
        p_sb = [[sb(f"p_sb{k}{i}", [128, 512], BF16) for i in range(2)]
                for k in range(2)]
        pvA_sb = sb("pvA_sb", [64, 1024], BF16)
        pvB_sb = sb("pvB_sb", [64, 1024], BF16)
        ao_sb = sb("ao_sb", [128, D], BF16)
        x1_sb = [sb(f"x1_sb{i}", [128, D], F32) for i in range(2)]
        xn2_sb = sb("xn2_sb", [128, D], BF16)
        xn2T_sb = sb("xn2T_sb", [128, D], BF16)
        t1_sb = sb("t1_sb", [128, D], F32)
        rp1_sb = [sb(f"rp1_{i}", [128, 1], F32) for i in range(2)]
        mr_sb = [sb(f"mr_{i}", [128, 1], F32) for i in range(2)]
        midg_sb = sb("midg_sb", [128, ER], BF16)
        midgT_sb = sb("midgT_sb", [128, ER], BF16)
        out_sb = [sb(f"out_sb{i}", [128, D], F32) for i in range(2)]
        g_sb = sb("g_sb", [128, E], F32)
        lg_sb = sb("lg_sb", [4, E], F32)
        eq_sb = sb("eq_sb", [4, E], F32)
        t2_sb = sb("t2_sb", [4, E], F32)
        keep_sb = sb("keep_sb", [4, E], F32)
        ee_sb = sb("ee_sb", [4, E], F32)
        ek_sb = sb("ek_sb", [4, E], F32)
        m1_sb = sb("m1_sb", [4, 1], F32)
        m1n_sb = sb("m1n_sb", [4, 1], F32)
        m2_sb = sb("m2_sb", [4, 1], F32)
        ss_sb = sb("ss_sb", [4, 1], F32)
        rs_sb = sb("rs_sb", [4, 1], F32)
        grow_sb = sb("grow_sb", [4, E], F32)
        ptA = ps("ptA", [128, 1024], BF16)
        ptQ = ps("ptQ", [64, 1024], BF16)
        psS = ps("psS", [128, 1024], F32)
        psV = ps("psV", [128, 1024], F32)
        psP = ps("psP", [128, 1024], F32)

        sems = {}
        for nm in ["tensor", "vector", "scalar", "gpsimd", "x0", "x1", "x2",
                   "x3", "o0", "o1", "wg", "ws", "gx0", "gxa", "gxb"]:
            sems[nm] = stack.enter_context(nc.semaphore(nm))
        block = stack.enter_context(nc.Block())

        em = S.emit

        wkeys = []

        def wload(engine, dst_ap, src_ap, key):
            if engine == "gpsimd":
                return em(engine, lambda e, d=dst_ap, s=src_ap: e.dma_start(
                    out=d, in_=s), writes=(key, "wgchain"), sem="wg", dma=True)
            wkeys.append(key)
            return em(engine, lambda e, d=dst_ap, s=src_ap: e.dma_start(
                out=d, in_=s), writes=(key,), sem="ws", dma=True)

        for key, dst, src in [("wq", wq_sb, wqh), ("wk", wk_sb, wkh)]:
            wload("gpsimd", dst[:].rearrange("p (c o) -> p c o", c=8),
                  src.rearrange("(c p) o -> p c o", p=128), key)
        wload("gpsimd", aa_sb[:].rearrange("p (c o) -> p c o", c=8),
              aah.rearrange("(c p) o -> p c o", p=128), "aall")
        wload("gpsimd", bw_sb[:].rearrange("p (c o) -> p c o", c=4),
              bwh.rearrange("(c p) o -> p c o", p=128), "bw")
        wload("gpsimd", bo_sb[:], boh, "bo")
        wload("sync", wr_sb[:].rearrange("p (c o) -> p c o", c=8),
              wrh.rearrange("(c p) o -> p c o", p=128), "wr")
        for key, dst, src in [("cq", cq_sb, cqh), ("sq", sq_sb, sqh),
                              ("ck", ck_sb, ckh), ("sk", sk_sb, skh),
                              ("idn", idn_sb, idnh), ("maskA", mA_sb, mAh),
                              ("maskB", mB_sb, mBh),
                              ("i4f", i4f_sb, i4fh),
                              ("i8f", i8f_sb, i8fh)]:
            wload("sync", dst[:], src, key)
        wb = em("vector", lambda e: e.engine_nop(), reads=tuple(wkeys))
        S.force_writer(wkeys, wb)
        em("vector", lambda e: e.memset(onesb[:], 1.0), writes=("onesb",))
        em("vector", lambda e: e.memset(onesf[:], 1.0), writes=("onesf",))
        em("vector", lambda e: e.memset(eps_sb[:], EPS), writes=("eps",))
        em("vector", lambda e: e.memset(nh_sb[:], -0.5), writes=("nhf",))
        em("vector", lambda e: e.memset(n1_sb[:], -1.0), writes=("n1f",))

        MM = dict(skip_group_check=True)
        NB = n_b

        def ln4():
            em("vector", lambda e: e.bn_stats(st6g_sb[0:NB, 0:6],
                                              row4a_sb[0:NB, 0:512]),
               reads=("row4a",), writes=("st6g",))
            em("vector", lambda e: e.bn_stats(st6g_sb[0:NB, 6:12],
                                              row4a_sb[0:NB, 512:1024]),
               reads=("row4a",), writes=("st6g",))
            em("vector", lambda e: e.bn_aggr(mvg_sb[0:NB, :],
                                             st6g_sb[0:NB, :]),
               reads=("st6g",), writes=("mvg",))
            em("scalar", lambda e: e.activation(
                lntg_sb[0:NB, :], mvg_sb[0:NB, 1:2], AF.Ln,
                bias=eps_sb[0:NB, :]),
               reads=("mvg", "eps"), writes=("lntg",))
            em("scalar", lambda e: e.activation(
                rstdg_sb[0:NB, :], lntg_sb[0:NB, :], AF.Exp,
                scale=nh_sb[0:NB, :]),
               reads=("lntg", "nhf"), writes=("rstdg",))
            em("vector", lambda e: e.tensor_scalar(
                row4b_sb[0:NB, :], row4a_sb[0:NB, :], mvg_sb[0:NB, 0:1],
                rstdg_sb[0:NB, :], ALU.subtract, ALU.mult),
               reads=("row4a", "mvg", "rstdg"), writes=("row4b",))

        def t4(dst_sb, dstk):
            for ci in range(8):
                em("tensor", lambda e, c=ci: e.matmul(
                    psS[:, c * NB:(c + 1) * NB],
                    row4b_sb[0:NB, c * 128:(c + 1) * 128],
                    i4f_sb[0:NB, 0:NB], start=True, stop=True, **MM),
                   reads=("row4b", "i4f"), writes=("psSa",))
            em("vector", lambda e, d=dst_sb: e.tensor_copy(
                d[:, 0:8 * NB], psS[:, 0:8 * NB]),
               reads=("psSa",), writes=(dstk,))

        def gates_v0():
            em("sync", lambda e: e.dma_start(out=row4a_sb[0:NB, :],
                                             in_=xq[:, 0, :]),
               writes=("row4a",), sem="gx0", dma=True)
            ln4()
            t4(xn0T_sb2, "xn0T2")
            for ci in range(8):
                pk = "ab"[ci % 2]
                em("sync", lambda e, c=ci: e.dma_start(
                    out=tmp32[c % 2][:], in_=wvh[c * 128:(c + 1) * 128, :]),
                   writes=(f"tmp32{pk}",), sem=f"gx{pk}", dma=True)
                em("vector", lambda e, c=ci: e.tensor_copy(
                    wv_sb[:, c * 1024:(c + 1) * 1024], tmp32[c % 2][:]),
                   reads=(f"tmp32{pk}",), writes=("wv",))
                for nh in range(2):
                    em("tensor", lambda e, c=ci, n=nh: e.matmul(
                        psS[0:NB, n * 512:(n + 1) * 512],
                        xn0T_sb2[:, c * NB:(c + 1) * NB],
                        tmp32[c % 2][:, n * 512:(n + 1) * 512],
                        start=(c == 0), stop=(c == 7), **MM),
                       reads=(f"tmp32{pk}", "xn0T2"),
                       writes=(f"psS{'ab'[nh]}",))
            em("vector", lambda e: e.tensor_copy(row4b_sb[0:NB, :],
                                                 psS[0:NB, 0:1024]),
               reads=("psSa", "psSb"), writes=("row4b",))
            t4(v0c_sb, "v0c")

        def gates_rest():
            for ci in range(8):
                pk = "ab"[ci % 2]
                em("sync", lambda e, c=ci: e.dma_start(
                    out=tmp32[c % 2][:], in_=woh[c * 128:(c + 1) * 128, :]),
                   writes=(f"tmp32{pk}",), sem=f"gx{pk}", dma=True)
                em("vector", lambda e, c=ci: e.tensor_copy(
                    wo_sb[:, c * 1024:(c + 1) * 1024], tmp32[c % 2][:]),
                   reads=(f"tmp32{pk}",), writes=("wo",))
                for nh in range(2):
                    em("tensor", lambda e, c=ci, n=nh: e.matmul(
                        psS[0:NB, n * 512:(n + 1) * 512],
                        v0c_sb[:, c * NB:(c + 1) * NB],
                        tmp32[c % 2][:, n * 512:(n + 1) * 512],
                        start=(c == 0), stop=False, **MM),
                       reads=(f"tmp32{pk}", "v0c"),
                       writes=(f"psS{'ab'[nh]}",))
            for nh in range(2):
                em("tensor", lambda e, n=nh: e.matmul(
                    psS[0:NB, n * 512:(n + 1) * 512], i4f_sb[0:NB, 0:NB],
                    row4a_sb[0:NB, n * 512:(n + 1) * 512],
                    start=False, stop=False, **MM),
                   reads=("i4f", "row4a"), writes=(f"psS{'ab'[nh]}",))
            em("sync", lambda e: e.dma_start(out=tmp32[0][0:1, :], in_=boh),
               writes=("tmp32a",), sem="gxa", dma=True)
            for nh in range(2):
                em("tensor", lambda e, n=nh: e.matmul(
                    psS[0:NB, n * 512:(n + 1) * 512], onesf[0:1, 0:NB],
                    tmp32[0][0:1, n * 512:(n + 1) * 512],
                    start=False, stop=True, **MM),
                   reads=("onesf", "tmp32a"), writes=(f"psS{'ab'[nh]}",))
            em("vector", lambda e: e.tensor_copy(row4a_sb[0:NB, :],
                                                 psS[0:NB, 0:1024]),
               reads=("psSa", "psSb"), writes=("row4a",))
            ln4()
            t4(xn20T_sb, "xn20T")
            for ci in range(8):
                em("tensor", lambda e, c=ci: e.matmul(
                    psS[0:NB, 512:512 + E],
                    xn20T_sb[:, c * NB:(c + 1) * NB],
                    wr_sb[:, c * E:(c + 1) * E],
                    start=(c == 0), stop=(c == 7), **MM),
                   reads=("xn20T", "wr"), writes=("psSb",))
            em("vector", lambda e: e.tensor_copy(lg_sb[0:NB, :],
                                                 psS[0:NB, 512:512 + E]),
               reads=("psSb",), writes=("lg",))
            em("vector", lambda e: e.reduce_max(m1_sb[0:NB, :],
                                                lg_sb[0:NB, :], axis=AX.X),
               reads=("lg",), writes=("m1",))
            em("vector", lambda e: e.tensor_single_scalar(
                eq_sb[0:NB, :], lg_sb[0:NB, :], m1_sb[0:NB, 0:1],
                ALU.is_equal),
               reads=("lg", "m1"), writes=("eq",))
            em("vector", lambda e: e.scalar_tensor_tensor(
                t2_sb[0:NB, :], eq_sb[0:NB, :], NEG, lg_sb[0:NB, :],
                ALU.mult, ALU.add),
               reads=("eq", "lg"), writes=("t2",))
            em("vector", lambda e: e.reduce_max(m2_sb[0:NB, :],
                                                t2_sb[0:NB, :], axis=AX.X),
               reads=("t2",), writes=("m2",))
            em("vector", lambda e: e.tensor_single_scalar(
                keep_sb[0:NB, :], lg_sb[0:NB, :], m2_sb[0:NB, 0:1],
                ALU.is_ge),
               reads=("lg", "m2"), writes=("keep",))
            em("vector", lambda e: e.tensor_scalar_mul(
                m1n_sb[0:NB, :], m1_sb[0:NB, :], -1.0),
               reads=("m1",), writes=("m1n",))
            em("scalar", lambda e: e.activation(
                ee_sb[0:NB, :], lg_sb[0:NB, :], AF.Exp,
                bias=m1n_sb[0:NB, 0:1]),
               reads=("lg", "m1n"), writes=("ee",))
            em("vector", lambda e: e.tensor_tensor(
                ek_sb[0:NB, :], ee_sb[0:NB, :], keep_sb[0:NB, :], ALU.mult),
               reads=("ee", "keep"), writes=("ek",))
            em("vector", lambda e: e.reduce_sum(ss_sb[0:NB, :],
                                                ek_sb[0:NB, :], axis=AX.X),
               reads=("ek",), writes=("ss",))
            em("vector", lambda e: e.reciprocal(rs_sb[0:NB, :],
                                                ss_sb[0:NB, :]),
               reads=("ss",), writes=("rs",))
            em("vector", lambda e: e.tensor_single_scalar(
                grow_sb[0:NB, :], ek_sb[0:NB, :], rs_sb[0:NB, 0:1],
                ALU.mult),
               reads=("ek", "rs"), writes=("grow",))
            em("tensor", lambda e: e.matmul(
                psS[0:E, 520:520 + NB], grow_sb[0:NB, :],
                i4f_sb[0:NB, 0:NB], start=True, stop=True, **MM),
               reads=("grow", "i4f"), writes=("psSb",))
            em("vector", lambda e: e.tensor_copy(growT_sb[:, 0:NB],
                                                 psS[0:E, 520:520 + NB]),
               reads=("psSb",), writes=("growT",))

        def ln(par, src_sb, src_key, out_ops):
            em("vector", lambda e: e.bn_stats(st6[par][:, 0:6],
                                             src_sb[:, 0:512]),
               reads=(src_key[0],), writes=(f"st6{par}",))
            em("vector", lambda e: e.bn_stats(st6[par][:, 6:12],
                                             src_sb[:, 512:1024]),
               reads=(src_key[1],), writes=(f"st6{par}",))
            em("vector", lambda e: e.bn_aggr(mv[par][:], st6[par][:]),
               reads=(f"st6{par}",), writes=(f"mv{par}",))
            em("scalar", lambda e: e.activation(lnt[par][:], mv[par][:, 1:2],
                                                AF.Ln, bias=eps_sb[:]),
               reads=(f"mv{par}", "eps"), writes=(f"lnt{par}",))
            em("scalar", lambda e: e.activation(rstd[par][:], lnt[par][:],
                                                AF.Exp, scale=nh_sb[:]),
               reads=(f"lnt{par}", "nhf"), writes=(f"rstd{par}",))
            for out_ap, out_key in out_ops:
                em("vector", lambda e, o=out_ap: e.tensor_scalar(
                    o, src_sb[:], mv[par][:, 0:1], rstd[par][:, 0:1],
                    ALU.subtract, ALU.mult),
                   reads=src_key + (f"mv{par}", f"rstd{par}"),
                   writes=(out_key,))

        def transpose8(dst_ps, dst_key, src_sb, src_key, nchunk=8):
            for ci in range(nchunk):
                em("tensor", lambda e, c=ci: e.transpose(
                    dst_ps[:, c * 128:(c + 1) * 128],
                    src_sb[:, c * 128:(c + 1) * 128], idn_sb[:]),
                   reads=(src_key, "idn"), writes=(dst_key,))
        NTT = n_b * n_t

        def tbj(t):
            b, j = divmod(t, n_t)
            return b, j, t & 1, j & 1

        def st_load(t):
            if t >= NTT:
                return
            b, j, par, jpar = tbj(t)
            xp = t % 4
            em("sync", lambda e, b_=b, j_=j, p=xp: e.dma_start(
                out=x_sb[p][:], in_=xq[b_, j_ * 128:(j_ + 1) * 128, :]),
               writes=(f"x{xp}",), sem=f"x{xp}", dma=True)

        def st_ln1(t):
            xp = t % 4
            src = x_sb[xp][:]
            em("vector", lambda e: e.bn_stats(st6L[:, 0:6], src[:, 0:512]),
               reads=(f"x{xp}",), writes=("st6L",))
            em("vector", lambda e: e.bn_stats(st6L[:, 6:12], src[:, 512:1024]),
               reads=(f"x{xp}",), writes=("st6L",))
            em("vector", lambda e: e.bn_aggr(mvL[:], st6L[:]),
               reads=("st6L",), writes=("mvL",))
            em("scalar", lambda e: e.activation(lntL[:], mvL[:, 1:2],
                                                AF.Ln, bias=eps_sb[:]),
               reads=("mvL", "eps"), writes=("lntL",))
            em("scalar", lambda e: e.activation(rstdL[:], lntL[:],
                                                AF.Exp, scale=nh_sb[:]),
               reads=("lntL", "nhf"), writes=("rstdL",))
            em("vector", lambda e: e.tensor_scalar(
                xn_sb[:], src, mvL[:, 0:1], rstdL[:, 0:1],
                ALU.subtract, ALU.mult),
               reads=(f"x{xp}", "mvL", "rstdL"), writes=("xn",))

        def st_xnT(t):
            transpose8(ptA, "ptA", xn_sb, "xn")
            em("vector", lambda e: e.tensor_copy(xnT_sb[:], ptA[:]),
               reads=("ptA",), writes=("xnT",))

        def proj_n(w_sb, wkey, nh2):
            for ci in range(8):
                em("tensor", lambda e, c=ci, n=nh2, w=w_sb: e.matmul(
                    psP[:, n * 512:(n + 1) * 512],
                    xnT_sb[:, c * 128:(c + 1) * 128],
                    w[:, c * 1024 + n * 512:c * 1024 + (n + 1) * 512],
                    start=(c == 0), stop=(c == 7), **MM),
                   reads=("xnT", wkey), writes=(f"psP{'ab'[nh2]}",))

        def rope(t, evac_sb, evk, ctab, stab, rot_sb, rotk):
            b, j, par, jpar = tbj(t)
            em("vector", lambda e, o=evac_sb: e.tensor_copy(o[:], psP[:]),
               reads=("psPa", "psPb"), writes=(evk,))
            c3 = ctab[:, j * 64:(j + 1) * 64].unsqueeze(1) \
                .broadcast_to([128, 16, 64])
            s4 = stab[:, j * 64:(j + 1) * 64] \
                .rearrange("p (a b) -> p a b", a=2).unsqueeze(1) \
                .broadcast_to([128, 16, 2, 32])
            p3 = evac_sb[:].rearrange("p (h d) -> p h d", h=16)
            p4s = evac_sb[:].rearrange("p (h a b) -> p h a b",
                                       h=16, a=2, b=32)[:, :, ::-1, :]
            a3 = a_sb[:].rearrange("p (h d) -> p h d", h=16)
            b4 = b_sb[:].rearrange("p (h a b) -> p h a b", h=16, a=2, b=32)
            em("vector", lambda e, o=a3, i=p3, cc=c3: e.tensor_tensor(
                o, i, cc, ALU.mult), reads=(evk,), writes=("a",))
            em("vector", lambda e, o=b4, i=p4s, ss=s4: e.tensor_tensor(
                o, i, ss, ALU.mult), reads=(evk,), writes=("b",))
            em("vector", lambda e, o=rot_sb: e.tensor_tensor(
                o[:], a_sb[:], b_sb[:], ALU.add),
               reads=("a", "b"), writes=(rotk,))

        def st_ropeq(t):
            rope(t, qsb_sb, "qsb", cq_sb, sq_sb, qrot_sb, "qrot")

        def st_ropek(t):
            rope(t, ksb_sb, "ksb", ck_sb, sk_sb, krot_sb, "krot")

        def st_vevac(t):
            b, j, par, jpar = tbj(t)
            em("vector", lambda e, jp=jpar: e.tensor_copy(
                v_sb[jp][:], psP[:]),
               reads=("psPa", "psPb"), writes=(f"v{jpar}",))

        def headT(rot_sb, rotk, dst_sb, dstk, half, ceng):
            for hh in range(8):
                h = half * 8 + hh
                em("tensor", lambda e, h_=h, o=hh * 128: e.transpose(
                    ptQ[0:64, o:o + 128],
                    rot_sb[:, h_ * 64:(h_ + 1) * 64], idn_sb[:]),
                   reads=(rotk, "idn"), writes=("ptQ",))
            if ceng == "scalar":
                em("scalar", lambda e, hf=half: e.activation(
                    dst_sb[0:64, hf * 1024:(hf + 1) * 1024], ptQ[0:64, :],
                    AF.Copy),
                   reads=("ptQ",), writes=(f"{dstk}{'ab'[half]}",))
            else:
                em("vector", lambda e, hf=half: e.tensor_copy(
                    dst_sb[0:64, hf * 1024:(hf + 1) * 1024], ptQ[0:64, :]),
                   reads=("ptQ",), writes=(f"{dstk}{'ab'[half]}",))

        def st_qT(t, half):
            headT(qrot_sb, "qrot", qT_sb, "qT", half, "scalar")

        def st_kT(t, half):
            b, j, par, jpar = tbj(t)
            headT(krot_sb, "krot", kT_sb[jpar], f"kT{jpar}", half, "vector")

        def kb_meta(t, kb):
            b, j, par, jpar = tbj(t)
            prev = kb == 0
            buf = (j - 1) & 1 if prev else jpar
            return dict(ktk=f"kT{buf}", vk=f"v{buf}",
                        ksrc=kT_sb[buf], vsrc=v_sb[buf],
                        msk=mA_sb if prev else mB_sb,
                        mkey="maskA" if prev else "maskB",
                        first=(j == 0) or (kb == 1 and False) or (kb == 0),
                        last=(kb == 1))

        def attn_scores(t, half, kb):
            m = kb_meta(t, kb)
            for q in range(2):
                qoff = q * 512
                for i4 in range(4):
                    h = half * 8 + q * 4 + i4
                    cols = slice(h * 128, h * 128 + 128)
                    em("tensor", lambda e, s_=qoff + i4 * 128,
                       cc=cols, st=(i4 == 0), ks=m["ksrc"]: e.matmul(
                           psS[:, s_:s_ + 128], ks[0:64, cc],
                           qT_sb[0:64, cc], start=st, stop=False, **MM),
                       reads=(m["ktk"] + "ab"[half], f"qT{'ab'[half]}"),
                       writes=(f"psS{'ab'[q]}",))
                em("tensor", lambda e, o=qoff, mm_=m["msk"]: e.matmul(
                    psS[:, o:o + 512], idn_sb[:], mm_[:],
                    start=False, stop=True, **MM),
                   reads=("idn", m["mkey"]), writes=(f"psS{'ab'[q]}",))
                em("scalar", lambda e, w_=p_sb[kb][q], o=qoff: e.activation(
                    w_[:], psS[:, o:o + 512], AF.Exp),
                   reads=(f"psS{'ab'[q]}",), writes=(f"p{kb}{q}",))

        def attn_pv(t, half, kb):
            b, j, par, jpar = tbj(t)
            m = kb_meta(t, kb)
            first = (kb == 0) or (j == 0)
            last = (kb == 1)
            for q in range(2):
                qoff = q * 512
                for i4 in range(4):
                    h = half * 8 + q * 4 + i4
                    em("tensor", lambda e, o=qoff + i4 * 128,
                       po=i4 * 128, vv=m["vsrc"], h_=h,
                       st=(first and i4 == 0),
                       sp=(last and i4 == 3), w_=p_sb[kb][q]: e.matmul(
                           psV[0:64, o:o + 128],
                           vv[:, h_ * 64:h_ * 64 + 64],
                           w_[:, po:po + 128],
                           start=st, stop=sp, **MM),
                       reads=(m["vk"], f"p{kb}{q}"),
                       writes=(f"psV{'ab'[q]}",))
                em("tensor", lambda e, o=qoff, w_=p_sb[kb][q],
                   st=first, sp=last: e.matmul(
                       psV[64:128, o:o + 512], onesb[:, 0:64],
                       w_[:, 0:512], start=st, stop=sp, **MM),
                   reads=("onesb", f"p{kb}{q}"), writes=(f"psV{'ab'[q]}",))

        def pv_evac(t, half):
            dst = pvA_sb if half == 0 else pvB_sb
            em("vector", lambda e, d=dst: e.tensor_copy(d[:], psV[0:64, :]),
               reads=("psVa", "psVb"), writes=(f"pv{half}",))

        def r_chain(t, half):
            em("scalar", lambda e: e.activation(
                lnz_sb[:], psV[64:65, :], AF.Ln),
               reads=("psVa", "psVb"), writes=("lnz",))
            em("scalar", lambda e, hf=half: e.activation(
                r2_sb[hf][:], lnz_sb[:], AF.Exp, scale=n1_sb[0:1, 0:1]),
               reads=("lnz", "n1f"), writes=(f"r{half}",))

        def rb_ao(t, half):
            for nh2 in range(2):
                em("tensor", lambda e, n=nh2, hf=half: e.matmul(
                    psS[0:64, n * 512:(n + 1) * 512],
                    onesb[0:1, 0:64],
                    r2_sb[hf][0:1, n * 512:(n + 1) * 512],
                    start=True, stop=True, **MM),
                   reads=("onesb", f"r{half}"), writes=(f"psS{'ab'[nh2]}",))
            pv_src = (pvA_sb if half == 0 else pvB_sb)[:]
            pv_e = pv_src.rearrange(
                "p (b two t) -> p b two t", b=4, two=2)[:, :, 0, :]
            pv_o = pv_src.rearrange(
                "p (b two t) -> p b two t", b=4, two=2)[:, :, 1, :]
            rb_e = psS[0:64, :].rearrange(
                "p (b two t) -> p b two t", b=4, two=2)[:, :, 0, :]
            rb_o = psS[0:64, :].rearrange(
                "p (b two t) -> p b two t", b=4, two=2)[:, :, 1, :]
            pv_keys = (f"pv{half}", "psSa", "psSb")
            ao_t = ao_sb[0:64, half * 512:(half + 1) * 512] \
                .rearrange("p (b t) -> p b t", b=4)
            ao_b = ao_sb[64:128, half * 512:(half + 1) * 512] \
                .rearrange("p (b t) -> p b t", b=4)
            em("vector", lambda e, o=ao_t, i0=pv_e, i1=rb_e:
               e.tensor_tensor(o, i0, i1, ALU.mult),
               reads=pv_keys, writes=(f"ao{half}",))
            em("vector", lambda e, o=ao_b, i0=pv_o, i1=rb_o:
               e.tensor_tensor(o, i0, i1, ALU.mult),
               reads=pv_keys, writes=(f"ao{half}",))

        def oproj_mms(part, nh2):
            for ci in range(part * 4, part * 4 + 4):
                em("tensor", lambda e, c=ci, n=nh2: e.matmul(
                    psS[:, n * 512:(n + 1) * 512],
                    ao_sb[:, c * 128:(c + 1) * 128],
                    wo_sb[:, c * 1024 + n * 512:c * 1024 + (n + 1) * 512],
                    start=(c == 0), stop=False, **MM),
                   reads=(f"ao{part}", "wo"), writes=(f"psS{'ab'[nh2]}",))

        def bo_mm(nh2):
            em("tensor", lambda e, n=nh2: e.matmul(
                psS[:, n * 512:(n + 1) * 512], onesb[0:1, :],
                bo_sb[0:1, n * 512:(n + 1) * 512],
                start=False, stop=True, **MM),
               reads=("onesb", "bo"), writes=(f"psS{'ab'[nh2]}",))

        def x1_half(t, nh2):
            b, j, par, jpar = tbj(t)
            xp = t % 4
            em("vector", lambda e, p=par, xp_=xp, n=nh2: e.tensor_tensor(
                x1_sb[p][:, n * 512:(n + 1) * 512],
                psS[:, n * 512:(n + 1) * 512],
                x_sb[xp_][:, n * 512:(n + 1) * 512], ALU.add),
               reads=(f"psS{'ab'[nh2]}", f"x{xp}"),
               writes=(f"x1{par}{'ab'[nh2]}",))

        def st_ln2a(t):
            b, j, par, jpar = tbj(t)
            em("vector", lambda e, p=par: e.bn_stats(
                st6[p][:, 0:6], x1_sb[p][:, 0:512]),
               reads=(f"x1{par}a",), writes=(f"st6{par}",))

        def st_ln2b(t):
            b, j, par, jpar = tbj(t)
            em("vector", lambda e, p=par: e.bn_stats(
                st6[p][:, 6:12], x1_sb[p][:, 512:1024]),
               reads=(f"x1{par}b",), writes=(f"st6{par}",))
            em("vector", lambda e, p=par: e.bn_aggr(mv[p][:], st6[p][:]),
               reads=(f"st6{par}",), writes=(f"mv{par}",))
            em("scalar", lambda e, p=par: e.activation(
                lnt[p][:], mv[p][:, 1:2], AF.Ln, bias=eps_sb[:]),
               reads=(f"mv{par}", "eps"), writes=(f"lnt{par}",))
            em("scalar", lambda e, p=par: e.activation(
                rstd[p][:], lnt[p][:], AF.Exp, scale=nh_sb[:]),
               reads=(f"lnt{par}", "nhf"), writes=(f"rstd{par}",))
            em("vector", lambda e, p=par: e.tensor_scalar(
                xn2_sb[:], x1_sb[p][:], mv[p][:, 0:1], rstd[p][:, 0:1],
                ALU.subtract, ALU.mult),
               reads=(f"x1{par}a", f"x1{par}b", f"mv{par}", f"rstd{par}"),
               writes=("xn2",))
            em("vector", lambda e, p=par: e.tensor_scalar_add(
                rp1_sb[p][:], rstd[p][:], 1.0),
               reads=(f"rstd{par}",), writes=(f"rp1{par}",))
            em("vector", lambda e, p=par: e.tensor_tensor(
                mr_sb[p][:], mv[p][:, 0:1], rstd[p][:], ALU.mult),
               reads=(f"mv{par}", f"rstd{par}"), writes=(f"mr{par}",))

        def st_xn2T(t):
            transpose8(ptA, "ptA", xn2_sb, "xn2")
            em("vector", lambda e: e.tensor_copy(xn2T_sb[:], ptA[:]),
               reads=("ptA",), writes=("xn2T",))

        def st_gates(t):
            b, j, par, jpar = tbj(t)
            em("tensor", lambda e, b_=b: e.matmul(
                psV[0:1, 608:608 + E], growT_sb[:, b_:b_ + 1], i8f_sb[:],
                start=True, stop=True, **MM),
               reads=("growT", "i8f"), writes=("psVb",))
            em("vector", lambda e: e.tensor_copy(growb_sb[:],
                                                 psV[0:1, 608:608 + E]),
               reads=("psVb",), writes=("growb",))
            em("tensor", lambda e: e.matmul(
                psV[:, 616:616 + E], onesf[:], growb_sb[:],
                start=True, stop=True, **MM),
               reads=("onesf", "growb"), writes=("psVb",))
            em("vector", lambda e: e.tensor_copy(g_sb[:], psV[:, 616:616 + E]),
               reads=("psVb",), writes=("g",))

        def st_mid(t):
            for ci in range(8):
                em("tensor", lambda e, c=ci: e.matmul(
                    psP[:, 0:512], xn2T_sb[:, c * 128:(c + 1) * 128],
                    aa_sb[:, c * 512:(c + 1) * 512],
                    start=(c == 0), stop=(c == 7), **MM),
                   reads=("xn2T", "aall"), writes=("psPa",))

        def st_midg(t):
            mid3 = psP[:, 0:512].rearrange("p (ee rr) -> p ee rr", ee=8)
            g3 = g_sb[:].unsqueeze(2).broadcast_to([128, 8, 64])
            mg3 = midg_sb[:].rearrange("p (ee rr) -> p ee rr", ee=8)
            em("vector", lambda e, o=mg3, i=mid3, gg=g3: e.tensor_tensor(
                o, i, gg, ALU.mult),
               reads=("psPa", "g"), writes=("midg",))
            transpose8(ptA, "ptA", midg_sb, "midg", nchunk=4)
            em("scalar", lambda e: e.activation(midgT_sb[:], ptA[:, 0:512],
                                                AF.Copy),
               reads=("ptA",), writes=("midgT",))

        def moeB_mms(nh2):
            for c2 in range(4):
                em("tensor", lambda e, c=c2, n=nh2: e.matmul(
                    psP[:, n * 512:(n + 1) * 512],
                    midgT_sb[:, c * 128:(c + 1) * 128],
                    bw_sb[:, c * 1024 + n * 512:c * 1024 + (n + 1) * 512],
                    start=(c == 0), stop=(c == 3), **MM),
                   reads=("midgT", "bw"), writes=(f"psP{'ab'[nh2]}",))

        def st_t1(t):
            b, j, par, jpar = tbj(t)
            em("vector", lambda e, p=par: e.tensor_scalar(
                t1_sb[:], x1_sb[p][:], rp1_sb[p][:], mr_sb[p][:],
                ALU.mult, ALU.subtract),
               reads=(f"x1{par}a", f"x1{par}b", f"rp1{par}", f"mr{par}"),
               writes=("t1",))

        def out_half(t, nh2):
            b, j, par, jpar = tbj(t)
            em("vector", lambda e, p=par, n=nh2: e.tensor_tensor(
                out_sb[p][:, n * 512:(n + 1) * 512],
                psP[:, n * 512:(n + 1) * 512],
                t1_sb[:, n * 512:(n + 1) * 512], ALU.add),
               reads=(f"psP{'ab'[nh2]}", "t1"), writes=(f"out{par}",))

        def st_dma_out(t):
            b, j, par, jpar = tbj(t)
            em("sync", lambda e, p=par, b_=b, j_=j: e.dma_start(
                out=outq[b_, j_ * 128:(j_ + 1) * 128, :], in_=out_sb[p][:]),
               reads=(f"out{par}",), sem=f"o{par}", dma=True)

        for t0 in range(3):
            st_load(t0)
        st_ln1(0); st_xnT(0)
        proj_n(wq_sb, "wq", 0); proj_n(wq_sb, "wq", 1)
        st_ropeq(0)
        proj_n(wk_sb, "wk", 0); proj_n(wk_sb, "wk", 1)
        st_ropek(0)
        st_qT(0, 0); st_qT(0, 1); st_kT(0, 0); st_kT(0, 1)
        gates_v0()
        proj_n(wv_sb, "wv", 0); proj_n(wv_sb, "wv", 1)
        st_vevac(0)
        st_ln1(1); st_xnT(1)
        attn_scores(0, 0, 1)
        attn_pv(0, 0, 1)
        pv_evac(0, 0)
        r_chain(0, 0)
        for t in range(NTT):
            n1 = t + 1 < NTT
            n2 = t + 2 < NTT
            b, j, par, jpar = tbj(t)
            j1 = (t + 1) % n_t
            st_load(t + 3)
            if n2:
                st_ln1(t + 2)
            if j > 0:
                attn_scores(t, 1, 0)
            if n1:
                proj_n(wq_sb, "wq", 0)
            attn_scores(t, 1, 1)
            if n1:
                proj_n(wq_sb, "wq", 1)
                st_ropeq(t + 1)
            if j > 0:
                attn_pv(t, 1, 0)
            if n1:
                proj_n(wk_sb, "wk", 0)
            attn_pv(t, 1, 1)
            if n1:
                proj_n(wk_sb, "wk", 1)
                st_ropek(t + 1)
            r_chain(t, 1)
            if t == 0:
                gates_rest()
            if n1:
                st_qT(t + 1, 0)
            rb_ao(t, 0)
            if n1:
                st_qT(t + 1, 1)
                st_kT(t + 1, 0)
            pv_evac(t, 1)
            rb_ao(t, 1)
            if n1:
                st_kT(t + 1, 1)
                proj_n(wv_sb, "wv", 0)
                proj_n(wv_sb, "wv", 1)
                st_vevac(t + 1)
            oproj_mms(0, 0)
            oproj_mms(0, 1)
            oproj_mms(1, 0)
            bo_mm(0)
            x1_half(t, 0)
            st_ln2a(t)
            oproj_mms(1, 1)
            bo_mm(1)
            x1_half(t, 1)
            st_ln2b(t)
            if j == 0:
                st_gates(t)
            if n1 and j1 > 0:
                attn_scores(t + 1, 0, 0)
            st_t1(t)
            if n1:
                attn_scores(t + 1, 0, 1)
            if n1 and j1 > 0:
                attn_pv(t + 1, 0, 0)
            st_xn2T(t)
            st_mid(t)
            if n1:
                attn_pv(t + 1, 0, 1)
            st_midg(t)
            moeB_mms(0)
            moeB_mms(1)
            out_half(t, 0)
            out_half(t, 1)
            st_dma_out(t)
            if n1:
                pv_evac(t + 1, 0)
                r_chain(t + 1, 0)
            if n2:
                st_xnT(t + 2)

        global _LAST_S
        _LAST_S = S
        if max_ops is not None:
            S.ops = S.ops[:max_ops]
        make_body = S.materialize(block, sems, None)
        finals = [(nm, S.totals.get(nm, 0)) for nm in ("o0", "o1")]
        block.sync(make_body("sync"))
        block.gpsimd(make_body("gpsimd", final_waits=finals))
        block.tensor(make_body("tensor"))
        block.vector(make_body("vector"))
        block.scalar(make_body("scalar"))
    return nc


_consts = _host_consts()


def _in_map(x_shard, wq, wk, wv, wo, bo, wr, aall, bwall):
    m = {
        "xq": np.ascontiguousarray(x_shard),
        "wqh": np.ascontiguousarray(wq.T),
        "wkh": np.ascontiguousarray(wk.T),
        "wvh": np.ascontiguousarray(wv.T),
        "woh": np.ascontiguousarray(wo.T),
        "boh": np.ascontiguousarray(bo.reshape(1, D)),
        "wrh": np.ascontiguousarray(wr.T),
        "aah": np.ascontiguousarray(aall),
        "bwh": np.ascontiguousarray(bwall),
        "cqh": _consts["cq"], "sqh": _consts["sq"],
        "ckh": _consts["ck"], "skh": _consts["sk"],
        "idnh": _consts["idn"], "mAh": _consts["maskA"],
        "mBh": _consts["maskB"],
        "i4fh": _consts["i4f"], "i8fh": _consts["i8f"],
    }
    return m


def kernel(L0_attn_query, L0_attn_Wq, L0_attn_Wk, L0_attn_Wv, L0_attn_Wo,
           L0_attn_bo, L0_ffn_Wr, L0_ffn_A, L0_ffn_Bw):
    x = np.asarray(L0_attn_query, dtype=np.float32)
    wq = np.asarray(L0_attn_Wq, np.float32)
    wk = np.asarray(L0_attn_Wk, np.float32)
    wv = np.asarray(L0_attn_Wv, np.float32)
    wo = np.asarray(L0_attn_Wo, np.float32)
    bo = np.asarray(L0_attn_bo, np.float32)
    wr = np.asarray(L0_ffn_Wr, np.float32)
    aall = np.ascontiguousarray(
        np.transpose(np.asarray(L0_ffn_A, np.float32), (2, 0, 1)).reshape(D, ER))
    bwall = np.ascontiguousarray(
        np.transpose(np.asarray(L0_ffn_Bw, np.float32), (0, 2, 1)).reshape(ER, D))

    nc = build_program(BPC, NT)
    in_maps = [_in_map(x[c * BPC:(c + 1) * BPC], wq, wk, wv, wo, bo, wr,
                       aall, bwall) for c in range(NCORE)]
    res = run_bass_kernel_spmd(nc, in_maps, core_ids=list(range(NCORE)))
    return np.concatenate([r["outq"] for r in res.results], axis=0)


# version 47
# speedup vs baseline: 1.0718x; 1.0718x over previous
import numpy as np
import ml_dtypes
from contextlib import ExitStack

import concourse.bass as bass
import concourse.mybir as mybir
from concourse.bass_utils import run_bass_kernel_spmd

F32 = mybir.dt.float32
BF16 = mybir.dt.bfloat16
AF = mybir.ActivationFunctionType
ALU = mybir.AluOpType
AX = mybir.AxisListType
bf16 = ml_dtypes.bfloat16

NCORE = 8
B, T, D = 32, 2048, 1024
NH, DH, WIN = 16, 64, 128
NT = T // 128
BPC = B // NCORE
E, R, ER = 8, 64, 512
NEG = -1.0e30
EPS = 1e-5


class Sched:

    def __init__(self):
        self.ops = []
        self.buf_w = {}
        self.buf_r = {}

    def emit(self, engine, fn, reads=(), writes=(), sem=None, dma=False):
        deps = set()
        for k in reads:
            w = self.buf_w.get(k)
            if w is not None:
                deps.add(w)
        for k in writes:
            w = self.buf_w.get(k)
            if w is not None:
                deps.add(w)
            for rid in self.buf_r.get(k, {}).values():
                deps.add(rid)
        oid = len(self.ops)
        self.ops.append(dict(engine=engine, fn=fn, sem=sem or engine,
                             deps=deps, dma=dma))
        for k in reads:
            self.buf_r.setdefault(k, {})[engine] = oid
        for k in writes:
            self.buf_w[k] = oid
            self.buf_r[k] = {}
        return oid

    def force_writer(self, keys, oid):
        for k in keys:
            self.buf_w[k] = oid
            self.buf_r[k] = {}

    def materialize(self, block, sems, engines):
        def keep(p, c):
            if (p["engine"] == c["engine"] == "tensor"
                    and not p["dma"] and not c["dma"]):
                return False
            return True

        needed = set()
        for op in self.ops:
            for d in op["deps"]:
                if keep(self.ops[d], op):
                    needed.add(d)
        counts = {}
        for oid, op in enumerate(self.ops):
            op["inc"] = op["dma"] or (oid in needed)
            if op["inc"]:
                q = 16 if op["dma"] else 1
                counts[op["sem"]] = counts.get(op["sem"], 0) + q
                op["val"] = counts[op["sem"]]
        for op in self.ops:
            waits = {}
            for d in op["deps"]:
                p = self.ops[d]
                if not keep(p, op):
                    continue
                waits[p["sem"]] = max(waits.get(p["sem"], 0), p["val"])
            op["waits"] = waits
        self.totals = counts

        def make_body(eng_name, final_waits=()):
            def body(eng):
                last = {}
                for op in self.ops:
                    if op["engine"] != eng_name:
                        continue
                    for snm, val in sorted(op["waits"].items()):
                        if val > last.get(snm, 0):
                            eng.wait_ge(sems[snm], val)
                            last[snm] = val
                    inst = op["fn"](eng)
                    if op["inc"]:
                        inst.then_inc(sems[op["sem"]], 16 if op["dma"] else 1)
                for snm, val in final_waits:
                    if val > last.get(snm, 0):
                        eng.wait_ge(sems[snm], val)
            return body
        return make_body


def _host_consts():
    inv = (np.float32(1.0) /
           np.float32(10000.0) ** (np.arange(0, DH, 2, dtype=np.float32)
                                   / np.float32(DH)))
    freqs = np.arange(T, dtype=np.float32)[:, None] * inv[None, :]
    emb = np.concatenate([freqs, freqs], axis=-1)
    cos = np.cos(emb).astype(np.float32)
    sin = np.sin(emb).astype(np.float32)
    sgn = np.concatenate([-np.ones(DH // 2, np.float32),
                          np.ones(DH // 2, np.float32)])
    ssin = sin * sgn[None, :]

    def tabify(t):
        return np.ascontiguousarray(
            t.reshape(NT, 128, DH).transpose(1, 0, 2).reshape(128, NT * DH))

    sm = np.float32(1.0 / np.sqrt(DH))
    r = np.arange(128)[:, None]
    c = np.arange(128)[None, :]
    maskA = np.tile(np.where(r > c, 0.0, NEG), (1, 4)).astype(bf16)
    maskB = np.tile(np.where(r <= c, 0.0, NEG), (1, 4)).astype(bf16)
    return {
        "cq": tabify(cos * sm).astype(bf16), "sq": tabify(ssin * sm).astype(bf16),
        "ck": tabify(cos).astype(bf16), "sk": tabify(ssin).astype(bf16),
        "idn": np.eye(128, dtype=np.float32).astype(bf16),
        "i4f": np.eye(4, dtype=np.float32),
        "i8f": np.eye(8, dtype=np.float32),
        "maskA": maskA, "maskB": maskB,
    }


_LAST_S = None


def build_program(n_b, n_t, max_ops=None):
    nc = bass.Bass("TRN2", target_bir_lowering=False)
    xq = nc.dram_tensor("xq", [n_b, T, D], F32, kind="ExternalInput").ap()
    wqh = nc.dram_tensor("wqh", [D, D], F32, kind="ExternalInput").ap()
    wkh = nc.dram_tensor("wkh", [D, D], F32, kind="ExternalInput").ap()
    wvh = nc.dram_tensor("wvh", [D, D], F32, kind="ExternalInput").ap()
    woh = nc.dram_tensor("woh", [D, D], F32, kind="ExternalInput").ap()
    boh = nc.dram_tensor("boh", [1, D], F32, kind="ExternalInput").ap()
    wrh = nc.dram_tensor("wrh", [D, E], F32, kind="ExternalInput").ap()
    aah = nc.dram_tensor("aah", [D, ER], F32, kind="ExternalInput").ap()
    bwh = nc.dram_tensor("bwh", [ER, D], F32, kind="ExternalInput").ap()
    cqh = nc.dram_tensor("cqh", [128, NT * DH], BF16, kind="ExternalInput").ap()
    sqh = nc.dram_tensor("sqh", [128, NT * DH], BF16, kind="ExternalInput").ap()
    ckh = nc.dram_tensor("ckh", [128, NT * DH], BF16, kind="ExternalInput").ap()
    skh = nc.dram_tensor("skh", [128, NT * DH], BF16, kind="ExternalInput").ap()
    idnh = nc.dram_tensor("idnh", [128, 128], BF16, kind="ExternalInput").ap()
    mAh = nc.dram_tensor("mAh", [128, 512], BF16, kind="ExternalInput").ap()
    mBh = nc.dram_tensor("mBh", [128, 512], BF16, kind="ExternalInput").ap()
    i4fh = nc.dram_tensor("i4fh", [4, 4], F32, kind="ExternalInput").ap()
    i8fh = nc.dram_tensor("i8fh", [8, 8], F32, kind="ExternalInput").ap()
    outq = nc.dram_tensor("outq", [n_b, T, D], F32, kind="ExternalOutput").ap()

    S = Sched()
    stack = ExitStack()
    with stack:
        def sb(name, shape, dt):
            return stack.enter_context(nc.sbuf_tensor(name, shape, dt))

        def ps(name, shape, dt):
            return stack.enter_context(nc.psum_tensor(name, shape, dt))

        wq_sb = sb("wq_sb", [128, 8 * D], BF16)
        wk_sb = sb("wk_sb", [128, 8 * D], BF16)
        wv_sb = sb("wv_sb", [128, 8 * D], BF16)
        wo_sb = sb("wo_sb", [128, 8 * D], BF16)
        aa_sb = sb("aa_sb", [128, 8 * ER], BF16)
        bw_sb = sb("bw_sb", [128, 4 * D], BF16)
        wr_sb = sb("wr_sb", [128, 8 * E], F32)
        bo_sb = sb("bo_sb", [1, D], BF16)
        cq_sb = sb("cq_sb", [128, NT * DH], BF16)
        sq_sb = sb("sq_sb", [128, NT * DH], BF16)
        ck_sb = sb("ck_sb", [128, NT * DH], BF16)
        sk_sb = sb("sk_sb", [128, NT * DH], BF16)
        idn_sb = sb("idn_sb", [128, 128], BF16)
        mA_sb = sb("mA_sb", [128, 512], BF16)
        mB_sb = sb("mB_sb", [128, 512], BF16)
        i4f_sb = sb("i4f_sb", [4, 4], F32)
        i8f_sb = sb("i8f_sb", [8, 8], F32)
        row4a_sb = sb("row4a_sb", [4, 1024], F32)
        row4b_sb = sb("row4b_sb", [4, 1024], F32)
        tmp32 = [sb(f"tmp32_{i}", [128, 1024], F32) for i in range(2)]
        xn0T_sb2 = sb("xn0T_sb2", [128, 32], F32)
        v0c_sb = sb("v0c_sb", [128, 32], F32)
        xn20T_sb = sb("xn20T_sb", [128, 32], F32)
        growT_sb = sb("growT_sb", [8, 4], F32)
        growb_sb = sb("growb_sb", [1, 8], F32)
        st6g_sb = sb("st6g_sb", [4, 12], F32)
        mvg_sb = sb("mvg_sb", [4, 2], F32)
        lntg_sb = sb("lntg_sb", [4, 1], F32)
        rstdg_sb = sb("rstdg_sb", [4, 1], F32)
        onesb = sb("onesb", [128, 128], BF16)
        onesf = sb("onesf", [1, 128], F32)
        eps_sb = sb("eps_sb", [128, 1], F32)
        nh_sb = sb("nh_sb", [128, 1], F32)
        n1_sb = sb("n1_sb", [128, 1], F32)
        x_sb = [sb(f"x_sb{i}", [128, D], F32) for i in range(4)]
        st6 = [sb(f"st6_{i}", [128, 12], F32) for i in range(2)]
        mv = [sb(f"mv_{i}", [128, 2], F32) for i in range(2)]
        lnt = [sb(f"lnt_{i}", [128, 1], F32) for i in range(2)]
        rstd = [sb(f"rstd_{i}", [128, 1], F32) for i in range(2)]
        st6L = sb("st6L", [128, 12], F32)
        mvL = sb("mvL", [128, 2], F32)
        lntL = sb("lntL", [128, 1], F32)
        rstdL = sb("rstdL", [128, 1], F32)
        xn_sb = sb("xn_sb", [128, D], BF16)
        xnT_sb = sb("xnT_sb", [128, D], BF16)
        a_sb = sb("a_sb", [128, D], BF16)
        b_sb = sb("b_sb", [128, D], BF16)
        qsb_sb = sb("qsb_sb", [128, D], BF16)
        ksb_sb = sb("ksb_sb", [128, D], BF16)
        qrot_sb = sb("qrot_sb", [128, D], BF16)
        krot_sb = sb("krot_sb", [128, D], BF16)
        qT_sb = sb("qT_sb", [64, 2 * D], BF16)
        lnz_sb = sb("lnz_sb", [1, 1024], F32)
        r2_sb = [sb(f"r2_sb{i}", [1, 1024], BF16) for i in range(2)]
        kT_sb = [sb(f"kT_sb{i}", [64, 2 * D], BF16) for i in range(2)]
        v_sb = [sb(f"v_sb{i}", [128, D], BF16) for i in range(2)]
        p_sb = [[sb(f"p_sb{k}{i}", [128, 512], BF16) for i in range(2)]
                for k in range(2)]
        pvA_sb = sb("pvA_sb", [64, 1024], BF16)
        pvB_sb = sb("pvB_sb", [64, 1024], BF16)
        ao_sb = sb("ao_sb", [128, D], BF16)
        x1_sb = [sb(f"x1_sb{i}", [128, D], F32) for i in range(2)]
        xn2_sb = sb("xn2_sb", [128, D], BF16)
        xn2T_sb = sb("xn2T_sb", [128, D], BF16)
        t1_sb = sb("t1_sb", [128, D], F32)
        rp1_sb = [sb(f"rp1_{i}", [128, 1], F32) for i in range(2)]
        mr_sb = [sb(f"mr_{i}", [128, 1], F32) for i in range(2)]
        midg_sb = sb("midg_sb", [128, ER], BF16)
        midgT_sb = sb("midgT_sb", [128, ER], BF16)
        out_sb = [sb(f"out_sb{i}", [128, D], F32) for i in range(2)]
        g_sb = sb("g_sb", [128, E], F32)
        lg_sb = sb("lg_sb", [4, E], F32)
        eq_sb = sb("eq_sb", [4, E], F32)
        t2_sb = sb("t2_sb", [4, E], F32)
        keep_sb = sb("keep_sb", [4, E], F32)
        ee_sb = sb("ee_sb", [4, E], F32)
        ek_sb = sb("ek_sb", [4, E], F32)
        m1_sb = sb("m1_sb", [4, 1], F32)
        m1n_sb = sb("m1n_sb", [4, 1], F32)
        m2_sb = sb("m2_sb", [4, 1], F32)
        ss_sb = sb("ss_sb", [4, 1], F32)
        rs_sb = sb("rs_sb", [4, 1], F32)
        grow_sb = sb("grow_sb", [4, E], F32)
        ptA = ps("ptA", [128, 1024], BF16)
        ptQ = ps("ptQ", [64, 1024], BF16)
        psS = ps("psS", [128, 1024], F32)
        psV = ps("psV", [128, 1024], F32)
        psP = ps("psP", [128, 1024], F32)

        sems = {}
        for nm in ["tensor", "vector", "scalar", "gpsimd", "x0", "x1", "x2",
                   "x3", "o0", "o1", "wg", "ws", "gx0", "gxa", "gxb"]:
            sems[nm] = stack.enter_context(nc.semaphore(nm))
        block = stack.enter_context(nc.Block())

        em = S.emit

        wkeys = []

        def wload(engine, dst_ap, src_ap, key):
            if engine == "gpsimd":
                return em(engine, lambda e, d=dst_ap, s=src_ap: e.dma_start(
                    out=d, in_=s), writes=(key, "wgchain"), sem="wg", dma=True)
            wkeys.append(key)
            return em(engine, lambda e, d=dst_ap, s=src_ap: e.dma_start(
                out=d, in_=s), writes=(key,), sem="ws", dma=True)

        for key, dst, src in [("wq", wq_sb, wqh), ("wk", wk_sb, wkh)]:
            wload("gpsimd", dst[:].rearrange("p (c o) -> p c o", c=8),
                  src.rearrange("(c p) o -> p c o", p=128), key)
        wload("gpsimd", aa_sb[:].rearrange("p (c o) -> p c o", c=8),
              aah.rearrange("(c p) o -> p c o", p=128), "aall")
        wload("gpsimd", bw_sb[:].rearrange("p (c o) -> p c o", c=4),
              bwh.rearrange("(c p) o -> p c o", p=128), "bw")
        wload("gpsimd", bo_sb[:], boh, "bo")
        wload("sync", wr_sb[:].rearrange("p (c o) -> p c o", c=8),
              wrh.rearrange("(c p) o -> p c o", p=128), "wr")
        for key, dst, src in [("cq", cq_sb, cqh), ("sq", sq_sb, sqh),
                              ("ck", ck_sb, ckh), ("sk", sk_sb, skh),
                              ("idn", idn_sb, idnh), ("maskA", mA_sb, mAh),
                              ("maskB", mB_sb, mBh),
                              ("i4f", i4f_sb, i4fh),
                              ("i8f", i8f_sb, i8fh)]:
            wload("sync", dst[:], src, key)
        wb = em("vector", lambda e: e.engine_nop(), reads=tuple(wkeys))
        S.force_writer(wkeys, wb)
        em("vector", lambda e: e.memset(onesb[:], 1.0), writes=("onesb",))
        em("vector", lambda e: e.memset(onesf[:], 1.0), writes=("onesf",))
        em("vector", lambda e: e.memset(eps_sb[:], EPS), writes=("eps",))
        em("vector", lambda e: e.memset(nh_sb[:], -0.5), writes=("nhf",))
        em("vector", lambda e: e.memset(n1_sb[:], -1.0), writes=("n1f",))

        MM = dict(skip_group_check=True)
        NB = n_b

        def ln4():
            em("vector", lambda e: e.bn_stats(st6g_sb[0:NB, 0:6],
                                              row4a_sb[0:NB, 0:512]),
               reads=("row4a",), writes=("st6g",))
            em("vector", lambda e: e.bn_stats(st6g_sb[0:NB, 6:12],
                                              row4a_sb[0:NB, 512:1024]),
               reads=("row4a",), writes=("st6g",))
            em("vector", lambda e: e.bn_aggr(mvg_sb[0:NB, :],
                                             st6g_sb[0:NB, :]),
               reads=("st6g",), writes=("mvg",))
            em("scalar", lambda e: e.activation(
                lntg_sb[0:NB, :], mvg_sb[0:NB, 1:2], AF.Ln,
                bias=eps_sb[0:NB, :]),
               reads=("mvg", "eps"), writes=("lntg",))
            em("scalar", lambda e: e.activation(
                rstdg_sb[0:NB, :], lntg_sb[0:NB, :], AF.Exp,
                scale=nh_sb[0:NB, :]),
               reads=("lntg", "nhf"), writes=("rstdg",))
            em("vector", lambda e: e.tensor_scalar(
                row4b_sb[0:NB, :], row4a_sb[0:NB, :], mvg_sb[0:NB, 0:1],
                rstdg_sb[0:NB, :], ALU.subtract, ALU.mult),
               reads=("row4a", "mvg", "rstdg"), writes=("row4b",))

        def t4(dst_sb, dstk):
            for ci in range(8):
                em("tensor", lambda e, c=ci: e.matmul(
                    psS[:, c * NB:(c + 1) * NB],
                    row4b_sb[0:NB, c * 128:(c + 1) * 128],
                    i4f_sb[0:NB, 0:NB], start=True, stop=True, **MM),
                   reads=("row4b", "i4f"), writes=("psSa",))
            em("vector", lambda e, d=dst_sb: e.tensor_copy(
                d[:, 0:8 * NB], psS[:, 0:8 * NB]),
               reads=("psSa",), writes=(dstk,))

        def gates_v0():
            em("sync", lambda e: e.dma_start(out=row4a_sb[0:NB, :],
                                             in_=xq[:, 0, :]),
               writes=("row4a",), sem="gx0", dma=True)
            ln4()
            t4(xn0T_sb2, "xn0T2")
            for ci in range(8):
                pk = "ab"[ci % 2]
                em("sync", lambda e, c=ci: e.dma_start(
                    out=tmp32[c % 2][:], in_=wvh[c * 128:(c + 1) * 128, :]),
                   writes=(f"tmp32{pk}",), sem=f"gx{pk}", dma=True)
                em("vector", lambda e, c=ci: e.tensor_copy(
                    wv_sb[:, c * 1024:(c + 1) * 1024], tmp32[c % 2][:]),
                   reads=(f"tmp32{pk}",), writes=("wv",))
                for nh in range(2):
                    em("tensor", lambda e, c=ci, n=nh: e.matmul(
                        psS[0:NB, n * 512:(n + 1) * 512],
                        xn0T_sb2[:, c * NB:(c + 1) * NB],
                        tmp32[c % 2][:, n * 512:(n + 1) * 512],
                        start=(c == 0), stop=(c == 7), **MM),
                       reads=(f"tmp32{pk}", "xn0T2"),
                       writes=(f"psS{'ab'[nh]}",))
            em("vector", lambda e: e.tensor_copy(row4b_sb[0:NB, :],
                                                 psS[0:NB, 0:1024]),
               reads=("psSa", "psSb"), writes=("row4b",))
            t4(v0c_sb, "v0c")

        def gates_rest():
            for ci in range(8):
                pk = "ab"[ci % 2]
                em("sync", lambda e, c=ci: e.dma_start(
                    out=tmp32[c % 2][:], in_=woh[c * 128:(c + 1) * 128, :]),
                   writes=(f"tmp32{pk}",), sem=f"gx{pk}", dma=True)
                em("vector", lambda e, c=ci: e.tensor_copy(
                    wo_sb[:, c * 1024:(c + 1) * 1024], tmp32[c % 2][:]),
                   reads=(f"tmp32{pk}",), writes=("wo",))
                for nh in range(2):
                    em("tensor", lambda e, c=ci, n=nh: e.matmul(
                        psS[0:NB, n * 512:(n + 1) * 512],
                        v0c_sb[:, c * NB:(c + 1) * NB],
                        tmp32[c % 2][:, n * 512:(n + 1) * 512],
                        start=(c == 0), stop=False, **MM),
                       reads=(f"tmp32{pk}", "v0c"),
                       writes=(f"psS{'ab'[nh]}",))
            for nh in range(2):
                em("tensor", lambda e, n=nh: e.matmul(
                    psS[0:NB, n * 512:(n + 1) * 512], i4f_sb[0:NB, 0:NB],
                    row4a_sb[0:NB, n * 512:(n + 1) * 512],
                    start=False, stop=False, **MM),
                   reads=("i4f", "row4a"), writes=(f"psS{'ab'[nh]}",))
            em("sync", lambda e: e.dma_start(out=tmp32[0][0:1, :], in_=boh),
               writes=("tmp32a",), sem="gxa", dma=True)
            for nh in range(2):
                em("tensor", lambda e, n=nh: e.matmul(
                    psS[0:NB, n * 512:(n + 1) * 512], onesf[0:1, 0:NB],
                    tmp32[0][0:1, n * 512:(n + 1) * 512],
                    start=False, stop=True, **MM),
                   reads=("onesf", "tmp32a"), writes=(f"psS{'ab'[nh]}",))
            em("vector", lambda e: e.tensor_copy(row4a_sb[0:NB, :],
                                                 psS[0:NB, 0:1024]),
               reads=("psSa", "psSb"), writes=("row4a",))
            ln4()
            t4(xn20T_sb, "xn20T")
            for ci in range(8):
                em("tensor", lambda e, c=ci: e.matmul(
                    psS[0:NB, 512:512 + E],
                    xn20T_sb[:, c * NB:(c + 1) * NB],
                    wr_sb[:, c * E:(c + 1) * E],
                    start=(c == 0), stop=(c == 7), **MM),
                   reads=("xn20T", "wr"), writes=("psSb",))
            em("vector", lambda e: e.tensor_copy(lg_sb[0:NB, :],
                                                 psS[0:NB, 512:512 + E]),
               reads=("psSb",), writes=("lg",))
            em("vector", lambda e: e.reduce_max(m1_sb[0:NB, :],
                                                lg_sb[0:NB, :], axis=AX.X),
               reads=("lg",), writes=("m1",))
            em("vector", lambda e: e.tensor_single_scalar(
                eq_sb[0:NB, :], lg_sb[0:NB, :], m1_sb[0:NB, 0:1],
                ALU.is_equal),
               reads=("lg", "m1"), writes=("eq",))
            em("vector", lambda e: e.scalar_tensor_tensor(
                t2_sb[0:NB, :], eq_sb[0:NB, :], NEG, lg_sb[0:NB, :],
                ALU.mult, ALU.add),
               reads=("eq", "lg"), writes=("t2",))
            em("vector", lambda e: e.reduce_max(m2_sb[0:NB, :],
                                                t2_sb[0:NB, :], axis=AX.X),
               reads=("t2",), writes=("m2",))
            em("vector", lambda e: e.tensor_single_scalar(
                keep_sb[0:NB, :], lg_sb[0:NB, :], m2_sb[0:NB, 0:1],
                ALU.is_ge),
               reads=("lg", "m2"), writes=("keep",))
            em("vector", lambda e: e.tensor_scalar_mul(
                m1n_sb[0:NB, :], m1_sb[0:NB, :], -1.0),
               reads=("m1",), writes=("m1n",))
            em("scalar", lambda e: e.activation(
                ee_sb[0:NB, :], lg_sb[0:NB, :], AF.Exp,
                bias=m1n_sb[0:NB, 0:1]),
               reads=("lg", "m1n"), writes=("ee",))
            em("vector", lambda e: e.tensor_tensor(
                ek_sb[0:NB, :], ee_sb[0:NB, :], keep_sb[0:NB, :], ALU.mult),
               reads=("ee", "keep"), writes=("ek",))
            em("vector", lambda e: e.reduce_sum(ss_sb[0:NB, :],
                                                ek_sb[0:NB, :], axis=AX.X),
               reads=("ek",), writes=("ss",))
            em("vector", lambda e: e.reciprocal(rs_sb[0:NB, :],
                                                ss_sb[0:NB, :]),
               reads=("ss",), writes=("rs",))
            em("vector", lambda e: e.tensor_single_scalar(
                grow_sb[0:NB, :], ek_sb[0:NB, :], rs_sb[0:NB, 0:1],
                ALU.mult),
               reads=("ek", "rs"), writes=("grow",))
            em("tensor", lambda e: e.matmul(
                psS[0:E, 520:520 + NB], grow_sb[0:NB, :],
                i4f_sb[0:NB, 0:NB], start=True, stop=True, **MM),
               reads=("grow", "i4f"), writes=("psSb",))
            em("vector", lambda e: e.tensor_copy(growT_sb[:, 0:NB],
                                                 psS[0:E, 520:520 + NB]),
               reads=("psSb",), writes=("growT",))

        def ln(par, src_sb, src_key, out_ops):
            em("vector", lambda e: e.bn_stats(st6[par][:, 0:6],
                                             src_sb[:, 0:512]),
               reads=(src_key[0],), writes=(f"st6{par}",))
            em("vector", lambda e: e.bn_stats(st6[par][:, 6:12],
                                             src_sb[:, 512:1024]),
               reads=(src_key[1],), writes=(f"st6{par}",))
            em("vector", lambda e: e.bn_aggr(mv[par][:], st6[par][:]),
               reads=(f"st6{par}",), writes=(f"mv{par}",))
            em("scalar", lambda e: e.activation(lnt[par][:], mv[par][:, 1:2],
                                                AF.Ln, bias=eps_sb[:]),
               reads=(f"mv{par}", "eps"), writes=(f"lnt{par}",))
            em("scalar", lambda e: e.activation(rstd[par][:], lnt[par][:],
                                                AF.Exp, scale=nh_sb[:]),
               reads=(f"lnt{par}", "nhf"), writes=(f"rstd{par}",))
            for out_ap, out_key in out_ops:
                em("vector", lambda e, o=out_ap: e.tensor_scalar(
                    o, src_sb[:], mv[par][:, 0:1], rstd[par][:, 0:1],
                    ALU.subtract, ALU.mult),
                   reads=src_key + (f"mv{par}", f"rstd{par}"),
                   writes=(out_key,))

        def transpose8(dst_ps, dst_key, src_sb, src_key, nchunk=8):
            for ci in range(nchunk):
                em("tensor", lambda e, c=ci: e.transpose(
                    dst_ps[:, c * 128:(c + 1) * 128],
                    src_sb[:, c * 128:(c + 1) * 128], idn_sb[:]),
                   reads=(src_key, "idn"), writes=(dst_key,))
        NTT = n_b * n_t

        def tbj(t):
            b, j = divmod(t, n_t)
            return b, j, t & 1, j & 1

        def st_load(t):
            if t >= NTT:
                return
            b, j, par, jpar = tbj(t)
            xp = t % 4
            em("sync", lambda e, b_=b, j_=j, p=xp: e.dma_start(
                out=x_sb[p][:], in_=xq[b_, j_ * 128:(j_ + 1) * 128, :]),
               writes=(f"x{xp}",), sem=f"x{xp}", dma=True)

        def st_ln1(t):
            xp = t % 4
            src = x_sb[xp][:]
            em("vector", lambda e: e.bn_stats(st6L[:, 0:6], src[:, 0:512]),
               reads=(f"x{xp}",), writes=("st6L",))
            em("vector", lambda e: e.bn_stats(st6L[:, 6:12], src[:, 512:1024]),
               reads=(f"x{xp}",), writes=("st6L",))
            em("vector", lambda e: e.bn_aggr(mvL[:], st6L[:]),
               reads=("st6L",), writes=("mvL",))
            em("scalar", lambda e: e.activation(lntL[:], mvL[:, 1:2],
                                                AF.Ln, bias=eps_sb[:]),
               reads=("mvL", "eps"), writes=("lntL",))
            em("scalar", lambda e: e.activation(rstdL[:], lntL[:],
                                                AF.Exp, scale=nh_sb[:]),
               reads=("lntL", "nhf"), writes=("rstdL",))
            em("vector", lambda e: e.tensor_scalar(
                xn_sb[:], src, mvL[:, 0:1], rstdL[:, 0:1],
                ALU.subtract, ALU.mult),
               reads=(f"x{xp}", "mvL", "rstdL"), writes=("xn",))

        def st_xnT(t):
            transpose8(ptA, "ptA", xn_sb, "xn")
            em("vector", lambda e: e.tensor_copy(xnT_sb[:], ptA[:]),
               reads=("ptA",), writes=("xnT",))

        def proj_n(w_sb, wkey, nh2):
            for ci in range(8):
                em("tensor", lambda e, c=ci, n=nh2, w=w_sb: e.matmul(
                    psP[:, n * 512:(n + 1) * 512],
                    xnT_sb[:, c * 128:(c + 1) * 128],
                    w[:, c * 1024 + n * 512:c * 1024 + (n + 1) * 512],
                    start=(c == 0), stop=(c == 7), **MM),
                   reads=("xnT", wkey), writes=(f"psP{'ab'[nh2]}",))

        def rope(t, evac_sb, evk, ctab, stab, rot_sb, rotk):
            b, j, par, jpar = tbj(t)
            em("vector", lambda e, o=evac_sb: e.tensor_copy(o[:], psP[:]),
               reads=("psPa", "psPb"), writes=(evk,))
            c3 = ctab[:, j * 64:(j + 1) * 64].unsqueeze(1) \
                .broadcast_to([128, 16, 64])
            s4 = stab[:, j * 64:(j + 1) * 64] \
                .rearrange("p (a b) -> p a b", a=2).unsqueeze(1) \
                .broadcast_to([128, 16, 2, 32])
            p3 = evac_sb[:].rearrange("p (h d) -> p h d", h=16)
            p4s = evac_sb[:].rearrange("p (h a b) -> p h a b",
                                       h=16, a=2, b=32)[:, :, ::-1, :]
            a3 = a_sb[:].rearrange("p (h d) -> p h d", h=16)
            b4 = b_sb[:].rearrange("p (h a b) -> p h a b", h=16, a=2, b=32)
            em("vector", lambda e, o=a3, i=p3, cc=c3: e.tensor_tensor(
                o, i, cc, ALU.mult), reads=(evk,), writes=("a",))
            em("vector", lambda e, o=b4, i=p4s, ss=s4: e.tensor_tensor(
                o, i, ss, ALU.mult), reads=(evk,), writes=("b",))
            em("vector", lambda e, o=rot_sb: e.tensor_tensor(
                o[:], a_sb[:], b_sb[:], ALU.add),
               reads=("a", "b"), writes=(rotk,))

        def st_ropeq(t):
            rope(t, qsb_sb, "qsb", cq_sb, sq_sb, qrot_sb, "qrot")

        def st_ropek(t):
            rope(t, ksb_sb, "ksb", ck_sb, sk_sb, krot_sb, "krot")

        def st_vevac(t):
            b, j, par, jpar = tbj(t)
            em("vector", lambda e, jp=jpar: e.tensor_copy(
                v_sb[jp][:], psP[:]),
               reads=("psPa", "psPb"), writes=(f"v{jpar}",))

        def headT(rot_sb, rotk, dst_sb, dstk, half, ceng):
            for hh in range(8):
                h = half * 8 + hh
                em("tensor", lambda e, h_=h, o=hh * 128: e.transpose(
                    ptQ[0:64, o:o + 128],
                    rot_sb[:, h_ * 64:(h_ + 1) * 64], idn_sb[:]),
                   reads=(rotk, "idn"), writes=("ptQ",))
            if ceng == "scalar":
                em("scalar", lambda e, hf=half: e.activation(
                    dst_sb[0:64, hf * 1024:(hf + 1) * 1024], ptQ[0:64, :],
                    AF.Copy),
                   reads=("ptQ",), writes=(f"{dstk}{'ab'[half]}",))
            else:
                em("vector", lambda e, hf=half: e.tensor_copy(
                    dst_sb[0:64, hf * 1024:(hf + 1) * 1024], ptQ[0:64, :]),
                   reads=("ptQ",), writes=(f"{dstk}{'ab'[half]}",))

        def st_qT(t, half):
            headT(qrot_sb, "qrot", qT_sb, "qT", half, "scalar")

        def st_kT(t, half):
            b, j, par, jpar = tbj(t)
            headT(krot_sb, "krot", kT_sb[jpar], f"kT{jpar}", half, "vector")

        def kb_meta(t, kb):
            b, j, par, jpar = tbj(t)
            prev = kb == 0
            buf = (j - 1) & 1 if prev else jpar
            return dict(ktk=f"kT{buf}", vk=f"v{buf}",
                        ksrc=kT_sb[buf], vsrc=v_sb[buf],
                        msk=mA_sb if prev else mB_sb,
                        mkey="maskA" if prev else "maskB",
                        first=(j == 0) or (kb == 1 and False) or (kb == 0),
                        last=(kb == 1))

        def attn_scores(t, half, kb):
            m = kb_meta(t, kb)
            for q in range(2):
                qoff = q * 512
                for i4 in range(4):
                    h = half * 8 + q * 4 + i4
                    cols = slice(h * 128, h * 128 + 128)
                    em("tensor", lambda e, s_=qoff + i4 * 128,
                       cc=cols, st=(i4 == 0), ks=m["ksrc"]: e.matmul(
                           psS[:, s_:s_ + 128], ks[0:64, cc],
                           qT_sb[0:64, cc], start=st, stop=False, **MM),
                       reads=(m["ktk"] + "ab"[half], f"qT{'ab'[half]}"),
                       writes=(f"psS{'ab'[q]}",))
                em("tensor", lambda e, o=qoff, mm_=m["msk"]: e.matmul(
                    psS[:, o:o + 512], idn_sb[:], mm_[:],
                    start=False, stop=True, **MM),
                   reads=("idn", m["mkey"]), writes=(f"psS{'ab'[q]}",))
                em("scalar", lambda e, w_=p_sb[kb][q], o=qoff: e.activation(
                    w_[:], psS[:, o:o + 512], AF.Exp),
                   reads=(f"psS{'ab'[q]}",), writes=(f"p{kb}{q}",))

        def attn_pv(t, half, kb):
            b, j, par, jpar = tbj(t)
            m = kb_meta(t, kb)
            first = (kb == 0) or (j == 0)
            last = (kb == 1)
            for q in range(2):
                qoff = q * 512
                for i4 in range(4):
                    h = half * 8 + q * 4 + i4
                    em("tensor", lambda e, o=qoff + i4 * 128,
                       po=i4 * 128, vv=m["vsrc"], h_=h,
                       st=(first and i4 == 0),
                       sp=(last and i4 == 3), w_=p_sb[kb][q]: e.matmul(
                           psV[0:64, o:o + 128],
                           vv[:, h_ * 64:h_ * 64 + 64],
                           w_[:, po:po + 128],
                           start=st, stop=sp, **MM),
                       reads=(m["vk"], f"p{kb}{q}"),
                       writes=(f"psV{'ab'[q]}",))
                em("tensor", lambda e, o=qoff, w_=p_sb[kb][q],
                   st=first, sp=last: e.matmul(
                       psV[64:128, o:o + 512], onesb[:, 0:64],
                       w_[:, 0:512], start=st, stop=sp, **MM),
                   reads=("onesb", f"p{kb}{q}"), writes=(f"psV{'ab'[q]}",))

        def pv_evac(t, half):
            dst = pvA_sb if half == 0 else pvB_sb
            em("vector", lambda e, d=dst: e.tensor_copy(d[:], psV[0:64, :]),
               reads=("psVa", "psVb"), writes=(f"pv{half}",))

        def r_chain(t, half):
            em("scalar", lambda e: e.activation(
                lnz_sb[:], psV[64:65, :], AF.Ln),
               reads=("psVa", "psVb"), writes=("lnz",))
            em("scalar", lambda e, hf=half: e.activation(
                r2_sb[hf][:], lnz_sb[:], AF.Exp, scale=n1_sb[0:1, 0:1]),
               reads=("lnz", "n1f"), writes=(f"r{half}",))

        def rb_ao(t, half):
            for nh2 in range(2):
                em("tensor", lambda e, n=nh2, hf=half: e.matmul(
                    psS[0:64, n * 512:(n + 1) * 512],
                    onesb[0:1, 0:64],
                    r2_sb[hf][0:1, n * 512:(n + 1) * 512],
                    start=True, stop=True, **MM),
                   reads=("onesb", f"r{half}"), writes=(f"psS{'ab'[nh2]}",))
            pv_src = (pvA_sb if half == 0 else pvB_sb)[:]
            pv_e = pv_src.rearrange(
                "p (b two t) -> p b two t", b=4, two=2)[:, :, 0, :]
            pv_o = pv_src.rearrange(
                "p (b two t) -> p b two t", b=4, two=2)[:, :, 1, :]
            rb_e = psS[0:64, :].rearrange(
                "p (b two t) -> p b two t", b=4, two=2)[:, :, 0, :]
            rb_o = psS[0:64, :].rearrange(
                "p (b two t) -> p b two t", b=4, two=2)[:, :, 1, :]
            pv_keys = (f"pv{half}", "psSa", "psSb")
            ao_t = ao_sb[0:64, half * 512:(half + 1) * 512] \
                .rearrange("p (b t) -> p b t", b=4)
            ao_b = ao_sb[64:128, half * 512:(half + 1) * 512] \
                .rearrange("p (b t) -> p b t", b=4)
            em("vector", lambda e, o=ao_t, i0=pv_e, i1=rb_e:
               e.tensor_tensor(o, i0, i1, ALU.mult),
               reads=pv_keys, writes=(f"ao{half}",))
            em("vector", lambda e, o=ao_b, i0=pv_o, i1=rb_o:
               e.tensor_tensor(o, i0, i1, ALU.mult),
               reads=pv_keys, writes=(f"ao{half}",))

        def oproj_mms(part, nh2):
            for ci in range(part * 4, part * 4 + 4):
                em("tensor", lambda e, c=ci, n=nh2: e.matmul(
                    psS[:, n * 512:(n + 1) * 512],
                    ao_sb[:, c * 128:(c + 1) * 128],
                    wo_sb[:, c * 1024 + n * 512:c * 1024 + (n + 1) * 512],
                    start=(c == 0), stop=False, **MM),
                   reads=(f"ao{part}", "wo"), writes=(f"psS{'ab'[nh2]}",))

        def bo_mm(nh2):
            em("tensor", lambda e, n=nh2: e.matmul(
                psS[:, n * 512:(n + 1) * 512], onesb[0:1, :],
                bo_sb[0:1, n * 512:(n + 1) * 512],
                start=False, stop=True, **MM),
               reads=("onesb", "bo"), writes=(f"psS{'ab'[nh2]}",))

        def x1_half(t, nh2):
            b, j, par, jpar = tbj(t)
            xp = t % 4
            em("vector", lambda e, p=par, xp_=xp, n=nh2: e.tensor_tensor(
                x1_sb[p][:, n * 512:(n + 1) * 512],
                psS[:, n * 512:(n + 1) * 512],
                x_sb[xp_][:, n * 512:(n + 1) * 512], ALU.add),
               reads=(f"psS{'ab'[nh2]}", f"x{xp}"),
               writes=(f"x1{par}{'ab'[nh2]}",))

        def st_ln2a(t):
            b, j, par, jpar = tbj(t)
            em("vector", lambda e, p=par: e.bn_stats(
                st6[p][:, 0:6], x1_sb[p][:, 0:512]),
               reads=(f"x1{par}a",), writes=(f"st6{par}",))

        def st_ln2b(t):
            b, j, par, jpar = tbj(t)
            em("vector", lambda e, p=par: e.bn_stats(
                st6[p][:, 6:12], x1_sb[p][:, 512:1024]),
               reads=(f"x1{par}b",), writes=(f"st6{par}",))
            em("vector", lambda e, p=par: e.bn_aggr(mv[p][:], st6[p][:]),
               reads=(f"st6{par}",), writes=(f"mv{par}",))
            em("scalar", lambda e, p=par: e.activation(
                lnt[p][:], mv[p][:, 1:2], AF.Ln, bias=eps_sb[:]),
               reads=(f"mv{par}", "eps"), writes=(f"lnt{par}",))
            em("scalar", lambda e, p=par: e.activation(
                rstd[p][:], lnt[p][:], AF.Exp, scale=nh_sb[:]),
               reads=(f"lnt{par}", "nhf"), writes=(f"rstd{par}",))
            em("vector", lambda e, p=par: e.tensor_scalar(
                xn2_sb[:], x1_sb[p][:], mv[p][:, 0:1], rstd[p][:, 0:1],
                ALU.subtract, ALU.mult),
               reads=(f"x1{par}a", f"x1{par}b", f"mv{par}", f"rstd{par}"),
               writes=("xn2",))
            em("vector", lambda e, p=par: e.tensor_scalar_add(
                rp1_sb[p][:], rstd[p][:], 1.0),
               reads=(f"rstd{par}",), writes=(f"rp1{par}",))
            em("vector", lambda e, p=par: e.tensor_tensor(
                mr_sb[p][:], mv[p][:, 0:1], rstd[p][:], ALU.mult),
               reads=(f"mv{par}", f"rstd{par}"), writes=(f"mr{par}",))

        def st_xn2T(t):
            transpose8(ptA, "ptA", xn2_sb, "xn2")
            em("vector", lambda e: e.tensor_copy(xn2T_sb[:], ptA[:]),
               reads=("ptA",), writes=("xn2T",))

        def st_gates(t):
            b, j, par, jpar = tbj(t)
            em("tensor", lambda e, b_=b: e.matmul(
                psV[0:1, 608:608 + E], growT_sb[:, b_:b_ + 1], i8f_sb[:],
                start=True, stop=True, **MM),
               reads=("growT", "i8f"), writes=("psVb",))
            em("vector", lambda e: e.tensor_copy(growb_sb[:],
                                                 psV[0:1, 608:608 + E]),
               reads=("psVb",), writes=("growb",))
            em("tensor", lambda e: e.matmul(
                psV[:, 616:616 + E], onesf[:], growb_sb[:],
                start=True, stop=True, **MM),
               reads=("onesf", "growb"), writes=("psVb",))
            em("vector", lambda e: e.tensor_copy(g_sb[:], psV[:, 616:616 + E]),
               reads=("psVb",), writes=("g",))

        def st_mid(t):
            for ci in range(8):
                em("tensor", lambda e, c=ci: e.matmul(
                    psP[:, 0:512], xn2T_sb[:, c * 128:(c + 1) * 128],
                    aa_sb[:, c * 512:(c + 1) * 512],
                    start=(c == 0), stop=(c == 7), **MM),
                   reads=("xn2T", "aall"), writes=("psPa",))

        def st_midg(t):
            mid3 = psP[:, 0:512].rearrange("p (ee rr) -> p ee rr", ee=8)
            g3 = g_sb[:].unsqueeze(2).broadcast_to([128, 8, 64])
            mg3 = midg_sb[:].rearrange("p (ee rr) -> p ee rr", ee=8)
            em("vector", lambda e, o=mg3, i=mid3, gg=g3: e.tensor_tensor(
                o, i, gg, ALU.mult),
               reads=("psPa", "g"), writes=("midg",))
            transpose8(ptA, "ptA", midg_sb, "midg", nchunk=4)
            em("scalar", lambda e: e.activation(midgT_sb[:], ptA[:, 0:512],
                                                AF.Copy),
               reads=("ptA",), writes=("midgT",))

        def moeB_mms(nh2):
            for c2 in range(4):
                em("tensor", lambda e, c=c2, n=nh2: e.matmul(
                    psP[:, n * 512:(n + 1) * 512],
                    midgT_sb[:, c * 128:(c + 1) * 128],
                    bw_sb[:, c * 1024 + n * 512:c * 1024 + (n + 1) * 512],
                    start=(c == 0), stop=(c == 3), **MM),
                   reads=("midgT", "bw"), writes=(f"psP{'ab'[nh2]}",))

        def st_t1(t):
            b, j, par, jpar = tbj(t)
            em("vector", lambda e, p=par: e.tensor_scalar(
                t1_sb[:], x1_sb[p][:], rp1_sb[p][:], mr_sb[p][:],
                ALU.mult, ALU.subtract),
               reads=(f"x1{par}a", f"x1{par}b", f"rp1{par}", f"mr{par}"),
               writes=("t1",))

        def out_half(t, nh2):
            b, j, par, jpar = tbj(t)
            em("vector", lambda e, p=par, n=nh2: e.tensor_tensor(
                out_sb[p][:, n * 512:(n + 1) * 512],
                psP[:, n * 512:(n + 1) * 512],
                t1_sb[:, n * 512:(n + 1) * 512], ALU.add),
               reads=(f"psP{'ab'[nh2]}", "t1"), writes=(f"out{par}",))

        def st_dma_out(t):
            b, j, par, jpar = tbj(t)
            em("sync", lambda e, p=par, b_=b, j_=j: e.dma_start(
                out=outq[b_, j_ * 128:(j_ + 1) * 128, :], in_=out_sb[p][:]),
               reads=(f"out{par}",), sem=f"o{par}", dma=True)

        for t0 in range(3):
            st_load(t0)
        st_ln1(0); st_xnT(0)
        proj_n(wq_sb, "wq", 0); proj_n(wq_sb, "wq", 1)
        st_ropeq(0)
        proj_n(wk_sb, "wk", 0); proj_n(wk_sb, "wk", 1)
        st_ropek(0)
        st_qT(0, 0); st_qT(0, 1); st_kT(0, 0); st_kT(0, 1)
        gates_v0()
        proj_n(wv_sb, "wv", 0); proj_n(wv_sb, "wv", 1)
        st_vevac(0)
        st_ln1(1); st_xnT(1)
        attn_scores(0, 0, 1)
        attn_pv(0, 0, 1)
        pv_evac(0, 0)
        r_chain(0, 0)
        for t in range(NTT):
            n1 = t + 1 < NTT
            n2 = t + 2 < NTT
            b, j, par, jpar = tbj(t)
            j1 = (t + 1) % n_t
            st_load(t + 3)
            if n2:
                st_ln1(t + 2)
            if j > 0:
                attn_scores(t, 1, 0)
            if n1:
                proj_n(wq_sb, "wq", 0)
            attn_scores(t, 1, 1)
            if n1:
                proj_n(wq_sb, "wq", 1)
                st_ropeq(t + 1)
            if j > 0:
                attn_pv(t, 1, 0)
            if n1:
                proj_n(wk_sb, "wk", 0)
            attn_pv(t, 1, 1)
            if n1:
                proj_n(wk_sb, "wk", 1)
                st_ropek(t + 1)
            r_chain(t, 1)
            if t == 0:
                gates_rest()
            if n1:
                st_qT(t + 1, 0)
            rb_ao(t, 0)
            if n1:
                st_qT(t + 1, 1)
                st_kT(t + 1, 0)
            pv_evac(t, 1)
            rb_ao(t, 1)
            if n1:
                st_kT(t + 1, 1)
                proj_n(wv_sb, "wv", 0)
                proj_n(wv_sb, "wv", 1)
                st_vevac(t + 1)
            oproj_mms(0, 0)
            oproj_mms(0, 1)
            oproj_mms(1, 0)
            bo_mm(0)
            x1_half(t, 0)
            st_ln2a(t)
            oproj_mms(1, 1)
            bo_mm(1)
            x1_half(t, 1)
            if j == 0:
                st_gates(t)
            if n1 and j1 > 0:
                attn_scores(t + 1, 0, 0)
            st_ln2b(t)
            if n1:
                attn_scores(t + 1, 0, 1)
            st_t1(t)
            if n1 and j1 > 0:
                attn_pv(t + 1, 0, 0)
            st_xn2T(t)
            st_mid(t)
            if n1:
                attn_pv(t + 1, 0, 1)
            st_midg(t)
            moeB_mms(0)
            moeB_mms(1)
            out_half(t, 0)
            out_half(t, 1)
            st_dma_out(t)
            if n1:
                pv_evac(t + 1, 0)
                r_chain(t + 1, 0)
            if n2:
                st_xnT(t + 2)

        global _LAST_S
        _LAST_S = S
        if max_ops is not None:
            S.ops = S.ops[:max_ops]
        make_body = S.materialize(block, sems, None)
        finals = [(nm, S.totals.get(nm, 0)) for nm in ("o0", "o1")]
        block.sync(make_body("sync"))
        block.gpsimd(make_body("gpsimd", final_waits=finals))
        block.tensor(make_body("tensor"))
        block.vector(make_body("vector"))
        block.scalar(make_body("scalar"))
    return nc


_consts = _host_consts()


def _in_map(x_shard, wq, wk, wv, wo, bo, wr, aall, bwall):
    m = {
        "xq": np.ascontiguousarray(x_shard),
        "wqh": np.ascontiguousarray(wq.T),
        "wkh": np.ascontiguousarray(wk.T),
        "wvh": np.ascontiguousarray(wv.T),
        "woh": np.ascontiguousarray(wo.T),
        "boh": np.ascontiguousarray(bo.reshape(1, D)),
        "wrh": np.ascontiguousarray(wr.T),
        "aah": np.ascontiguousarray(aall),
        "bwh": np.ascontiguousarray(bwall),
        "cqh": _consts["cq"], "sqh": _consts["sq"],
        "ckh": _consts["ck"], "skh": _consts["sk"],
        "idnh": _consts["idn"], "mAh": _consts["maskA"],
        "mBh": _consts["maskB"],
        "i4fh": _consts["i4f"], "i8fh": _consts["i8f"],
    }
    return m


def kernel(L0_attn_query, L0_attn_Wq, L0_attn_Wk, L0_attn_Wv, L0_attn_Wo,
           L0_attn_bo, L0_ffn_Wr, L0_ffn_A, L0_ffn_Bw):
    x = np.asarray(L0_attn_query, dtype=np.float32)
    wq = np.asarray(L0_attn_Wq, np.float32)
    wk = np.asarray(L0_attn_Wk, np.float32)
    wv = np.asarray(L0_attn_Wv, np.float32)
    wo = np.asarray(L0_attn_Wo, np.float32)
    bo = np.asarray(L0_attn_bo, np.float32)
    wr = np.asarray(L0_ffn_Wr, np.float32)
    aall = np.ascontiguousarray(
        np.transpose(np.asarray(L0_ffn_A, np.float32), (2, 0, 1)).reshape(D, ER))
    bwall = np.ascontiguousarray(
        np.transpose(np.asarray(L0_ffn_Bw, np.float32), (0, 2, 1)).reshape(ER, D))

    nc = build_program(BPC, NT)
    in_maps = [_in_map(x[c * BPC:(c + 1) * BPC], wq, wk, wv, wo, bo, wr,
                       aall, bwall) for c in range(NCORE)]
    res = run_bass_kernel_spmd(nc, in_maps, core_ids=list(range(NCORE)))
    return np.concatenate([r["outq"] for r in res.results], axis=0)


# version 49 (final)
# speedup vs baseline: 1.0738x; 1.0019x over previous
import numpy as np
import ml_dtypes
from contextlib import ExitStack

import concourse.bass as bass
import concourse.mybir as mybir
from concourse.bass_utils import run_bass_kernel_spmd

F32 = mybir.dt.float32
BF16 = mybir.dt.bfloat16
AF = mybir.ActivationFunctionType
ALU = mybir.AluOpType
AX = mybir.AxisListType
bf16 = ml_dtypes.bfloat16

NCORE = 8
B, T, D = 32, 2048, 1024
NH, DH, WIN = 16, 64, 128
NT = T // 128
BPC = B // NCORE
E, R, ER = 8, 64, 512
NEG = -1.0e30
EPS = 1e-5


class Sched:

    def __init__(self):
        self.ops = []
        self.buf_w = {}
        self.buf_r = {}

    def emit(self, engine, fn, reads=(), writes=(), sem=None, dma=False):
        deps = set()
        for k in reads:
            w = self.buf_w.get(k)
            if w is not None:
                deps.add(w)
        for k in writes:
            w = self.buf_w.get(k)
            if w is not None:
                deps.add(w)
            for rid in self.buf_r.get(k, {}).values():
                deps.add(rid)
        oid = len(self.ops)
        self.ops.append(dict(engine=engine, fn=fn, sem=sem or engine,
                             deps=deps, dma=dma))
        for k in reads:
            self.buf_r.setdefault(k, {})[engine] = oid
        for k in writes:
            self.buf_w[k] = oid
            self.buf_r[k] = {}
        return oid

    def force_writer(self, keys, oid):
        for k in keys:
            self.buf_w[k] = oid
            self.buf_r[k] = {}

    def materialize(self, block, sems, engines):
        def keep(p, c):
            if (p["engine"] == c["engine"] == "tensor"
                    and not p["dma"] and not c["dma"]):
                return False
            return True

        needed = set()
        for op in self.ops:
            for d in op["deps"]:
                if keep(self.ops[d], op):
                    needed.add(d)
        counts = {}
        for oid, op in enumerate(self.ops):
            op["inc"] = op["dma"] or (oid in needed)
            if op["inc"]:
                q = 16 if op["dma"] else 1
                counts[op["sem"]] = counts.get(op["sem"], 0) + q
                op["val"] = counts[op["sem"]]
        for op in self.ops:
            waits = {}
            for d in op["deps"]:
                p = self.ops[d]
                if not keep(p, op):
                    continue
                waits[p["sem"]] = max(waits.get(p["sem"], 0), p["val"])
            op["waits"] = waits
        self.totals = counts

        def make_body(eng_name, final_waits=()):
            def body(eng):
                last = {}
                for op in self.ops:
                    if op["engine"] != eng_name:
                        continue
                    for snm, val in sorted(op["waits"].items()):
                        if val > last.get(snm, 0):
                            eng.wait_ge(sems[snm], val)
                            last[snm] = val
                    inst = op["fn"](eng)
                    if op["inc"]:
                        inst.then_inc(sems[op["sem"]], 16 if op["dma"] else 1)
                for snm, val in final_waits:
                    if val > last.get(snm, 0):
                        eng.wait_ge(sems[snm], val)
            return body
        return make_body


def _host_consts():
    inv = (np.float32(1.0) /
           np.float32(10000.0) ** (np.arange(0, DH, 2, dtype=np.float32)
                                   / np.float32(DH)))
    freqs = np.arange(T, dtype=np.float32)[:, None] * inv[None, :]
    emb = np.concatenate([freqs, freqs], axis=-1)
    cos = np.cos(emb).astype(np.float32)
    sin = np.sin(emb).astype(np.float32)
    sgn = np.concatenate([-np.ones(DH // 2, np.float32),
                          np.ones(DH // 2, np.float32)])
    ssin = sin * sgn[None, :]

    def tabify(t):
        return np.ascontiguousarray(
            t.reshape(NT, 128, DH).transpose(1, 0, 2).reshape(128, NT * DH))

    sm = np.float32(1.0 / np.sqrt(DH))
    r = np.arange(128)[:, None]
    c = np.arange(128)[None, :]
    maskA = np.tile(np.where(r > c, 0.0, NEG), (1, 4)).astype(bf16)
    maskB = np.tile(np.where(r <= c, 0.0, NEG), (1, 4)).astype(bf16)
    return {
        "cq": tabify(cos * sm).astype(bf16), "sq": tabify(ssin * sm).astype(bf16),
        "ck": tabify(cos).astype(bf16), "sk": tabify(ssin).astype(bf16),
        "idn": np.eye(128, dtype=np.float32).astype(bf16),
        "i4f": np.eye(4, dtype=np.float32),
        "i8f": np.eye(8, dtype=np.float32),
        "maskA": maskA, "maskB": maskB,
    }


_LAST_S = None


def build_program(n_b, n_t, max_ops=None):
    nc = bass.Bass("TRN2", target_bir_lowering=False)
    xq = nc.dram_tensor("xq", [n_b, T, D], F32, kind="ExternalInput").ap()
    wqh = nc.dram_tensor("wqh", [D, D], F32, kind="ExternalInput").ap()
    wkh = nc.dram_tensor("wkh", [D, D], F32, kind="ExternalInput").ap()
    wvh = nc.dram_tensor("wvh", [D, D], F32, kind="ExternalInput").ap()
    woh = nc.dram_tensor("woh", [D, D], F32, kind="ExternalInput").ap()
    boh = nc.dram_tensor("boh", [1, D], F32, kind="ExternalInput").ap()
    wrh = nc.dram_tensor("wrh", [D, E], F32, kind="ExternalInput").ap()
    aah = nc.dram_tensor("aah", [D, ER], F32, kind="ExternalInput").ap()
    bwh = nc.dram_tensor("bwh", [ER, D], F32, kind="ExternalInput").ap()
    cqh = nc.dram_tensor("cqh", [128, NT * DH], BF16, kind="ExternalInput").ap()
    sqh = nc.dram_tensor("sqh", [128, NT * DH], BF16, kind="ExternalInput").ap()
    ckh = nc.dram_tensor("ckh", [128, NT * DH], BF16, kind="ExternalInput").ap()
    skh = nc.dram_tensor("skh", [128, NT * DH], BF16, kind="ExternalInput").ap()
    idnh = nc.dram_tensor("idnh", [128, 128], BF16, kind="ExternalInput").ap()
    mAh = nc.dram_tensor("mAh", [128, 512], BF16, kind="ExternalInput").ap()
    mBh = nc.dram_tensor("mBh", [128, 512], BF16, kind="ExternalInput").ap()
    i4fh = nc.dram_tensor("i4fh", [4, 4], F32, kind="ExternalInput").ap()
    i8fh = nc.dram_tensor("i8fh", [8, 8], F32, kind="ExternalInput").ap()
    outq = nc.dram_tensor("outq", [n_b, T, D], F32, kind="ExternalOutput").ap()

    S = Sched()
    stack = ExitStack()
    with stack:
        def sb(name, shape, dt):
            return stack.enter_context(nc.sbuf_tensor(name, shape, dt))

        def ps(name, shape, dt):
            return stack.enter_context(nc.psum_tensor(name, shape, dt))

        wq_sb = sb("wq_sb", [128, 8 * D], BF16)
        wk_sb = sb("wk_sb", [128, 8 * D], BF16)
        wv_sb = sb("wv_sb", [128, 8 * D], BF16)
        wo_sb = sb("wo_sb", [128, 8 * D], BF16)
        aa_sb = sb("aa_sb", [128, 8 * ER], BF16)
        bw_sb = sb("bw_sb", [128, 4 * D], BF16)
        wr_sb = sb("wr_sb", [128, 8 * E], F32)
        bo_sb = sb("bo_sb", [1, D], BF16)
        cq_sb = sb("cq_sb", [128, NT * DH], BF16)
        sq_sb = sb("sq_sb", [128, NT * DH], BF16)
        ck_sb = sb("ck_sb", [128, NT * DH], BF16)
        sk_sb = sb("sk_sb", [128, NT * DH], BF16)
        idn_sb = sb("idn_sb", [128, 128], BF16)
        mA_sb = sb("mA_sb", [128, 512], BF16)
        mB_sb = sb("mB_sb", [128, 512], BF16)
        i4f_sb = sb("i4f_sb", [4, 4], F32)
        i8f_sb = sb("i8f_sb", [8, 8], F32)
        row4a_sb = sb("row4a_sb", [4, 1024], F32)
        row4b_sb = sb("row4b_sb", [4, 1024], F32)
        tmp32 = [sb(f"tmp32_{i}", [128, 1024], F32) for i in range(2)]
        xn0T_sb2 = sb("xn0T_sb2", [128, 32], F32)
        v0c_sb = sb("v0c_sb", [128, 32], F32)
        xn20T_sb = sb("xn20T_sb", [128, 32], F32)
        growT_sb = sb("growT_sb", [8, 4], F32)
        growb_sb = sb("growb_sb", [1, 8], F32)
        st6g_sb = sb("st6g_sb", [4, 12], F32)
        mvg_sb = sb("mvg_sb", [4, 2], F32)
        lntg_sb = sb("lntg_sb", [4, 1], F32)
        rstdg_sb = sb("rstdg_sb", [4, 1], F32)
        onesb = sb("onesb", [128, 128], BF16)
        onesf = sb("onesf", [1, 128], F32)
        eps_sb = sb("eps_sb", [128, 1], F32)
        nh_sb = sb("nh_sb", [128, 1], F32)
        n1_sb = sb("n1_sb", [128, 1], F32)
        x_sb = [sb(f"x_sb{i}", [128, D], F32) for i in range(4)]
        st6 = [sb(f"st6_{i}", [128, 12], F32) for i in range(2)]
        mv = [sb(f"mv_{i}", [128, 2], F32) for i in range(2)]
        lnt = [sb(f"lnt_{i}", [128, 1], F32) for i in range(2)]
        rstd = [sb(f"rstd_{i}", [128, 1], F32) for i in range(2)]
        st6L = sb("st6L", [128, 12], F32)
        mvL = sb("mvL", [128, 2], F32)
        lntL = sb("lntL", [128, 1], F32)
        rstdL = sb("rstdL", [128, 1], F32)
        xn_sb = sb("xn_sb", [128, D], BF16)
        xnT_sb = sb("xnT_sb", [128, D], BF16)
        a_sb = sb("a_sb", [128, D], BF16)
        b_sb = sb("b_sb", [128, D], BF16)
        qsb_sb = sb("qsb_sb", [128, D], BF16)
        ksb_sb = sb("ksb_sb", [128, D], BF16)
        qrot_sb = sb("qrot_sb", [128, D], BF16)
        krot_sb = sb("krot_sb", [128, D], BF16)
        qT_sb = sb("qT_sb", [64, 2 * D], BF16)
        lnz_sb = sb("lnz_sb", [1, 1024], F32)
        r2_sb = [sb(f"r2_sb{i}", [1, 1024], BF16) for i in range(2)]
        kT_sb = [sb(f"kT_sb{i}", [64, 2 * D], BF16) for i in range(2)]
        v_sb = [sb(f"v_sb{i}", [128, D], BF16) for i in range(2)]
        p_sb = [[sb(f"p_sb{k}{i}", [128, 512], BF16) for i in range(2)]
                for k in range(2)]
        pvA_sb = sb("pvA_sb", [64, 1024], BF16)
        pvB_sb = sb("pvB_sb", [64, 1024], BF16)
        ao_sb = sb("ao_sb", [128, D], BF16)
        x1_sb = [sb(f"x1_sb{i}", [128, D], F32) for i in range(2)]
        xn2_sb = sb("xn2_sb", [128, D], BF16)
        xn2T_sb = sb("xn2T_sb", [128, D], BF16)
        t1_sb = sb("t1_sb", [128, D], F32)
        rp1_sb = [sb(f"rp1_{i}", [128, 1], F32) for i in range(2)]
        mr_sb = [sb(f"mr_{i}", [128, 1], F32) for i in range(2)]
        midg_sb = sb("midg_sb", [128, ER], BF16)
        midgT_sb = sb("midgT_sb", [128, ER], BF16)
        out_sb = [sb(f"out_sb{i}", [128, D], F32) for i in range(2)]
        g_sb = sb("g_sb", [128, E], F32)
        lg_sb = sb("lg_sb", [4, E], F32)
        eq_sb = sb("eq_sb", [4, E], F32)
        t2_sb = sb("t2_sb", [4, E], F32)
        keep_sb = sb("keep_sb", [4, E], F32)
        ee_sb = sb("ee_sb", [4, E], F32)
        ek_sb = sb("ek_sb", [4, E], F32)
        m1_sb = sb("m1_sb", [4, 1], F32)
        m1n_sb = sb("m1n_sb", [4, 1], F32)
        m2_sb = sb("m2_sb", [4, 1], F32)
        ss_sb = sb("ss_sb", [4, 1], F32)
        rs_sb = sb("rs_sb", [4, 1], F32)
        grow_sb = sb("grow_sb", [4, E], F32)
        ptA = ps("ptA", [128, 1024], BF16)
        ptQ = ps("ptQ", [64, 1024], BF16)
        psS = ps("psS", [128, 1024], F32)
        psV = ps("psV", [128, 1024], F32)
        psP = ps("psP", [128, 1024], F32)

        sems = {}
        for nm in ["tensor", "vector", "scalar", "gpsimd", "x0", "x1", "x2",
                   "x3", "o0", "o1", "wg", "ws", "gx0", "gxa", "gxb"]:
            sems[nm] = stack.enter_context(nc.semaphore(nm))
        block = stack.enter_context(nc.Block())

        em = S.emit

        wkeys = []

        def wload(engine, dst_ap, src_ap, key):
            if engine == "gpsimd":
                return em(engine, lambda e, d=dst_ap, s=src_ap: e.dma_start(
                    out=d, in_=s), writes=(key, "wgchain"), sem="wg", dma=True)
            wkeys.append(key)
            return em(engine, lambda e, d=dst_ap, s=src_ap: e.dma_start(
                out=d, in_=s), writes=(key,), sem="ws", dma=True)

        for key, dst, src in [("wq", wq_sb, wqh), ("wk", wk_sb, wkh)]:
            wload("gpsimd", dst[:].rearrange("p (c o) -> p c o", c=8),
                  src.rearrange("(c p) o -> p c o", p=128), key)
        wload("gpsimd", aa_sb[:].rearrange("p (c o) -> p c o", c=8),
              aah.rearrange("(c p) o -> p c o", p=128), "aall")
        wload("gpsimd", bw_sb[:].rearrange("p (c o) -> p c o", c=4),
              bwh.rearrange("(c p) o -> p c o", p=128), "bw")
        wload("gpsimd", bo_sb[:], boh, "bo")
        wload("sync", wr_sb[:].rearrange("p (c o) -> p c o", c=8),
              wrh.rearrange("(c p) o -> p c o", p=128), "wr")
        for key, dst, src in [("cq", cq_sb, cqh), ("sq", sq_sb, sqh),
                              ("ck", ck_sb, ckh), ("sk", sk_sb, skh),
                              ("idn", idn_sb, idnh), ("maskA", mA_sb, mAh),
                              ("maskB", mB_sb, mBh),
                              ("i4f", i4f_sb, i4fh),
                              ("i8f", i8f_sb, i8fh)]:
            wload("sync", dst[:], src, key)
        wb = em("vector", lambda e: e.engine_nop(), reads=tuple(wkeys))
        S.force_writer(wkeys, wb)
        em("vector", lambda e: e.memset(onesb[:], 1.0), writes=("onesb",))
        em("vector", lambda e: e.memset(onesf[:], 1.0), writes=("onesf",))
        em("vector", lambda e: e.memset(eps_sb[:], EPS), writes=("eps",))
        em("vector", lambda e: e.memset(nh_sb[:], -0.5), writes=("nhf",))
        em("vector", lambda e: e.memset(n1_sb[:], -1.0), writes=("n1f",))

        MM = dict(skip_group_check=True)
        NB = n_b

        def ln4():
            em("vector", lambda e: e.bn_stats(st6g_sb[0:NB, 0:6],
                                              row4a_sb[0:NB, 0:512]),
               reads=("row4a",), writes=("st6g",))
            em("vector", lambda e: e.bn_stats(st6g_sb[0:NB, 6:12],
                                              row4a_sb[0:NB, 512:1024]),
               reads=("row4a",), writes=("st6g",))
            em("vector", lambda e: e.bn_aggr(mvg_sb[0:NB, :],
                                             st6g_sb[0:NB, :]),
               reads=("st6g",), writes=("mvg",))
            em("scalar", lambda e: e.activation(
                lntg_sb[0:NB, :], mvg_sb[0:NB, 1:2], AF.Ln,
                bias=eps_sb[0:NB, :]),
               reads=("mvg", "eps"), writes=("lntg",))
            em("scalar", lambda e: e.activation(
                rstdg_sb[0:NB, :], lntg_sb[0:NB, :], AF.Exp,
                scale=nh_sb[0:NB, :]),
               reads=("lntg", "nhf"), writes=("rstdg",))
            em("vector", lambda e: e.tensor_scalar(
                row4b_sb[0:NB, :], row4a_sb[0:NB, :], mvg_sb[0:NB, 0:1],
                rstdg_sb[0:NB, :], ALU.subtract, ALU.mult),
               reads=("row4a", "mvg", "rstdg"), writes=("row4b",))

        def t4(dst_sb, dstk):
            for ci in range(8):
                em("tensor", lambda e, c=ci: e.matmul(
                    psS[:, c * NB:(c + 1) * NB],
                    row4b_sb[0:NB, c * 128:(c + 1) * 128],
                    i4f_sb[0:NB, 0:NB], start=True, stop=True, **MM),
                   reads=("row4b", "i4f"), writes=("psSa",))
            em("vector", lambda e, d=dst_sb: e.tensor_copy(
                d[:, 0:8 * NB], psS[:, 0:8 * NB]),
               reads=("psSa",), writes=(dstk,))

        def gates_v0():
            em("sync", lambda e: e.dma_start(out=row4a_sb[0:NB, :],
                                             in_=xq[:, 0, :]),
               writes=("row4a",), sem="gx0", dma=True)
            ln4()
            t4(xn0T_sb2, "xn0T2")
            for ci in range(8):
                pk = "ab"[ci % 2]
                em("sync", lambda e, c=ci: e.dma_start(
                    out=tmp32[c % 2][:], in_=wvh[c * 128:(c + 1) * 128, :]),
                   writes=(f"tmp32{pk}",), sem=f"gx{pk}", dma=True)
                em("vector", lambda e, c=ci: e.tensor_copy(
                    wv_sb[:, c * 1024:(c + 1) * 1024], tmp32[c % 2][:]),
                   reads=(f"tmp32{pk}",), writes=("wv",))
                for nh in range(2):
                    em("tensor", lambda e, c=ci, n=nh: e.matmul(
                        psS[0:NB, n * 512:(n + 1) * 512],
                        xn0T_sb2[:, c * NB:(c + 1) * NB],
                        tmp32[c % 2][:, n * 512:(n + 1) * 512],
                        start=(c == 0), stop=(c == 7), **MM),
                       reads=(f"tmp32{pk}", "xn0T2"),
                       writes=(f"psS{'ab'[nh]}",))
            em("vector", lambda e: e.tensor_copy(row4b_sb[0:NB, :],
                                                 psS[0:NB, 0:1024]),
               reads=("psSa", "psSb"), writes=("row4b",))
            t4(v0c_sb, "v0c")

        def gates_rest():
            for ci in range(8):
                pk = "ab"[ci % 2]
                em("sync", lambda e, c=ci: e.dma_start(
                    out=tmp32[c % 2][:], in_=woh[c * 128:(c + 1) * 128, :]),
                   writes=(f"tmp32{pk}",), sem=f"gx{pk}", dma=True)
                em("vector", lambda e, c=ci: e.tensor_copy(
                    wo_sb[:, c * 1024:(c + 1) * 1024], tmp32[c % 2][:]),
                   reads=(f"tmp32{pk}",), writes=("wo",))
                for nh in range(2):
                    em("tensor", lambda e, c=ci, n=nh: e.matmul(
                        psS[0:NB, n * 512:(n + 1) * 512],
                        v0c_sb[:, c * NB:(c + 1) * NB],
                        tmp32[c % 2][:, n * 512:(n + 1) * 512],
                        start=(c == 0), stop=False, **MM),
                       reads=(f"tmp32{pk}", "v0c"),
                       writes=(f"psS{'ab'[nh]}",))
            for nh in range(2):
                em("tensor", lambda e, n=nh: e.matmul(
                    psS[0:NB, n * 512:(n + 1) * 512], i4f_sb[0:NB, 0:NB],
                    row4a_sb[0:NB, n * 512:(n + 1) * 512],
                    start=False, stop=False, **MM),
                   reads=("i4f", "row4a"), writes=(f"psS{'ab'[nh]}",))
            em("sync", lambda e: e.dma_start(out=tmp32[0][0:1, :], in_=boh),
               writes=("tmp32a",), sem="gxa", dma=True)
            for nh in range(2):
                em("tensor", lambda e, n=nh: e.matmul(
                    psS[0:NB, n * 512:(n + 1) * 512], onesf[0:1, 0:NB],
                    tmp32[0][0:1, n * 512:(n + 1) * 512],
                    start=False, stop=True, **MM),
                   reads=("onesf", "tmp32a"), writes=(f"psS{'ab'[nh]}",))
            em("vector", lambda e: e.tensor_copy(row4a_sb[0:NB, :],
                                                 psS[0:NB, 0:1024]),
               reads=("psSa", "psSb"), writes=("row4a",))
            ln4()
            t4(xn20T_sb, "xn20T")
            for ci in range(8):
                em("tensor", lambda e, c=ci: e.matmul(
                    psS[0:NB, 512:512 + E],
                    xn20T_sb[:, c * NB:(c + 1) * NB],
                    wr_sb[:, c * E:(c + 1) * E],
                    start=(c == 0), stop=(c == 7), **MM),
                   reads=("xn20T", "wr"), writes=("psSb",))
            em("vector", lambda e: e.tensor_copy(lg_sb[0:NB, :],
                                                 psS[0:NB, 512:512 + E]),
               reads=("psSb",), writes=("lg",))
            em("vector", lambda e: e.reduce_max(m1_sb[0:NB, :],
                                                lg_sb[0:NB, :], axis=AX.X),
               reads=("lg",), writes=("m1",))
            em("vector", lambda e: e.tensor_single_scalar(
                eq_sb[0:NB, :], lg_sb[0:NB, :], m1_sb[0:NB, 0:1],
                ALU.is_equal),
               reads=("lg", "m1"), writes=("eq",))
            em("vector", lambda e: e.scalar_tensor_tensor(
                t2_sb[0:NB, :], eq_sb[0:NB, :], NEG, lg_sb[0:NB, :],
                ALU.mult, ALU.add),
               reads=("eq", "lg"), writes=("t2",))
            em("vector", lambda e: e.reduce_max(m2_sb[0:NB, :],
                                                t2_sb[0:NB, :], axis=AX.X),
               reads=("t2",), writes=("m2",))
            em("vector", lambda e: e.tensor_single_scalar(
                keep_sb[0:NB, :], lg_sb[0:NB, :], m2_sb[0:NB, 0:1],
                ALU.is_ge),
               reads=("lg", "m2"), writes=("keep",))
            em("vector", lambda e: e.tensor_scalar_mul(
                m1n_sb[0:NB, :], m1_sb[0:NB, :], -1.0),
               reads=("m1",), writes=("m1n",))
            em("scalar", lambda e: e.activation(
                ee_sb[0:NB, :], lg_sb[0:NB, :], AF.Exp,
                bias=m1n_sb[0:NB, 0:1]),
               reads=("lg", "m1n"), writes=("ee",))
            em("vector", lambda e: e.tensor_tensor(
                ek_sb[0:NB, :], ee_sb[0:NB, :], keep_sb[0:NB, :], ALU.mult),
               reads=("ee", "keep"), writes=("ek",))
            em("vector", lambda e: e.reduce_sum(ss_sb[0:NB, :],
                                                ek_sb[0:NB, :], axis=AX.X),
               reads=("ek",), writes=("ss",))
            em("vector", lambda e: e.reciprocal(rs_sb[0:NB, :],
                                                ss_sb[0:NB, :]),
               reads=("ss",), writes=("rs",))
            em("vector", lambda e: e.tensor_single_scalar(
                grow_sb[0:NB, :], ek_sb[0:NB, :], rs_sb[0:NB, 0:1],
                ALU.mult),
               reads=("ek", "rs"), writes=("grow",))
            em("tensor", lambda e: e.matmul(
                psS[0:E, 520:520 + NB], grow_sb[0:NB, :],
                i4f_sb[0:NB, 0:NB], start=True, stop=True, **MM),
               reads=("grow", "i4f"), writes=("psSb",))
            em("vector", lambda e: e.tensor_copy(growT_sb[:, 0:NB],
                                                 psS[0:E, 520:520 + NB]),
               reads=("psSb",), writes=("growT",))

        def ln(par, src_sb, src_key, out_ops):
            em("vector", lambda e: e.bn_stats(st6[par][:, 0:6],
                                             src_sb[:, 0:512]),
               reads=(src_key[0],), writes=(f"st6{par}",))
            em("vector", lambda e: e.bn_stats(st6[par][:, 6:12],
                                             src_sb[:, 512:1024]),
               reads=(src_key[1],), writes=(f"st6{par}",))
            em("vector", lambda e: e.bn_aggr(mv[par][:], st6[par][:]),
               reads=(f"st6{par}",), writes=(f"mv{par}",))
            em("scalar", lambda e: e.activation(lnt[par][:], mv[par][:, 1:2],
                                                AF.Ln, bias=eps_sb[:]),
               reads=(f"mv{par}", "eps"), writes=(f"lnt{par}",))
            em("scalar", lambda e: e.activation(rstd[par][:], lnt[par][:],
                                                AF.Exp, scale=nh_sb[:]),
               reads=(f"lnt{par}", "nhf"), writes=(f"rstd{par}",))
            for out_ap, out_key in out_ops:
                em("vector", lambda e, o=out_ap: e.tensor_scalar(
                    o, src_sb[:], mv[par][:, 0:1], rstd[par][:, 0:1],
                    ALU.subtract, ALU.mult),
                   reads=src_key + (f"mv{par}", f"rstd{par}"),
                   writes=(out_key,))

        def transpose8(dst_ps, dst_key, src_sb, src_key, nchunk=8):
            for ci in range(nchunk):
                em("tensor", lambda e, c=ci: e.transpose(
                    dst_ps[:, c * 128:(c + 1) * 128],
                    src_sb[:, c * 128:(c + 1) * 128], idn_sb[:]),
                   reads=(src_key, "idn"), writes=(dst_key,))
        NTT = n_b * n_t

        def tbj(t):
            b, j = divmod(t, n_t)
            return b, j, t & 1, j & 1

        def st_load(t):
            if t >= NTT:
                return
            b, j, par, jpar = tbj(t)
            xp = t % 4
            em("sync", lambda e, b_=b, j_=j, p=xp: e.dma_start(
                out=x_sb[p][:], in_=xq[b_, j_ * 128:(j_ + 1) * 128, :]),
               writes=(f"x{xp}",), sem=f"x{xp}", dma=True)

        def st_ln1(t):
            xp = t % 4
            src = x_sb[xp][:]
            em("vector", lambda e: e.bn_stats(st6L[:, 0:6], src[:, 0:512]),
               reads=(f"x{xp}",), writes=("st6L",))
            em("vector", lambda e: e.bn_stats(st6L[:, 6:12], src[:, 512:1024]),
               reads=(f"x{xp}",), writes=("st6L",))
            em("vector", lambda e: e.bn_aggr(mvL[:], st6L[:]),
               reads=("st6L",), writes=("mvL",))
            em("scalar", lambda e: e.activation(lntL[:], mvL[:, 1:2],
                                                AF.Ln, bias=eps_sb[:]),
               reads=("mvL", "eps"), writes=("lntL",))
            em("scalar", lambda e: e.activation(rstdL[:], lntL[:],
                                                AF.Exp, scale=nh_sb[:]),
               reads=("lntL", "nhf"), writes=("rstdL",))
            em("vector", lambda e: e.tensor_scalar(
                xn_sb[:], src, mvL[:, 0:1], rstdL[:, 0:1],
                ALU.subtract, ALU.mult),
               reads=(f"x{xp}", "mvL", "rstdL"), writes=("xn",))

        def st_xnT(t):
            transpose8(ptA, "ptA", xn_sb, "xn")
            em("vector", lambda e: e.tensor_copy(xnT_sb[:], ptA[:]),
               reads=("ptA",), writes=("xnT",))

        def proj_n(w_sb, wkey, nh2):
            for ci in range(8):
                em("tensor", lambda e, c=ci, n=nh2, w=w_sb: e.matmul(
                    psP[:, n * 512:(n + 1) * 512],
                    xnT_sb[:, c * 128:(c + 1) * 128],
                    w[:, c * 1024 + n * 512:c * 1024 + (n + 1) * 512],
                    start=(c == 0), stop=(c == 7), **MM),
                   reads=("xnT", wkey), writes=(f"psP{'ab'[nh2]}",))

        def rope(t, evac_sb, evk, ctab, stab, rot_sb, rotk):
            b, j, par, jpar = tbj(t)
            em("vector", lambda e, o=evac_sb: e.tensor_copy(o[:], psP[:]),
               reads=("psPa", "psPb"), writes=(evk,))
            c3 = ctab[:, j * 64:(j + 1) * 64].unsqueeze(1) \
                .broadcast_to([128, 16, 64])
            s4 = stab[:, j * 64:(j + 1) * 64] \
                .rearrange("p (a b) -> p a b", a=2).unsqueeze(1) \
                .broadcast_to([128, 16, 2, 32])
            p3 = evac_sb[:].rearrange("p (h d) -> p h d", h=16)
            p4s = evac_sb[:].rearrange("p (h a b) -> p h a b",
                                       h=16, a=2, b=32)[:, :, ::-1, :]
            a3 = a_sb[:].rearrange("p (h d) -> p h d", h=16)
            b4 = b_sb[:].rearrange("p (h a b) -> p h a b", h=16, a=2, b=32)
            em("vector", lambda e, o=a3, i=p3, cc=c3: e.tensor_tensor(
                o, i, cc, ALU.mult), reads=(evk,), writes=("a",))
            em("vector", lambda e, o=b4, i=p4s, ss=s4: e.tensor_tensor(
                o, i, ss, ALU.mult), reads=(evk,), writes=("b",))
            em("vector", lambda e, o=rot_sb: e.tensor_tensor(
                o[:], a_sb[:], b_sb[:], ALU.add),
               reads=("a", "b"), writes=(rotk,))

        def st_ropeq(t):
            rope(t, qsb_sb, "qsb", cq_sb, sq_sb, qrot_sb, "qrot")

        def st_ropek(t):
            rope(t, ksb_sb, "ksb", ck_sb, sk_sb, krot_sb, "krot")

        def st_vevac(t):
            b, j, par, jpar = tbj(t)
            em("scalar", lambda e, jp=jpar: e.activation(
                v_sb[jp][:], psP[:], AF.Copy),
               reads=("psPa", "psPb"), writes=(f"v{jpar}",))

        def headT(rot_sb, rotk, dst_sb, dstk, half, ceng):
            for hh in range(8):
                h = half * 8 + hh
                em("tensor", lambda e, h_=h, o=hh * 128: e.transpose(
                    ptQ[0:64, o:o + 128],
                    rot_sb[:, h_ * 64:(h_ + 1) * 64], idn_sb[:]),
                   reads=(rotk, "idn"), writes=("ptQ",))
            if ceng == "scalar":
                em("scalar", lambda e, hf=half: e.activation(
                    dst_sb[0:64, hf * 1024:(hf + 1) * 1024], ptQ[0:64, :],
                    AF.Copy),
                   reads=("ptQ",), writes=(f"{dstk}{'ab'[half]}",))
            else:
                em("vector", lambda e, hf=half: e.tensor_copy(
                    dst_sb[0:64, hf * 1024:(hf + 1) * 1024], ptQ[0:64, :]),
                   reads=("ptQ",), writes=(f"{dstk}{'ab'[half]}",))

        def st_qT(t, half):
            headT(qrot_sb, "qrot", qT_sb, "qT", half, "scalar")

        def st_kT(t, half):
            b, j, par, jpar = tbj(t)
            headT(krot_sb, "krot", kT_sb[jpar], f"kT{jpar}", half, "scalar")

        def kb_meta(t, kb):
            b, j, par, jpar = tbj(t)
            prev = kb == 0
            buf = (j - 1) & 1 if prev else jpar
            return dict(ktk=f"kT{buf}", vk=f"v{buf}",
                        ksrc=kT_sb[buf], vsrc=v_sb[buf],
                        msk=mA_sb if prev else mB_sb,
                        mkey="maskA" if prev else "maskB",
                        first=(j == 0) or (kb == 1 and False) or (kb == 0),
                        last=(kb == 1))

        def attn_scores(t, half, kb):
            m = kb_meta(t, kb)
            for q in range(2):
                qoff = q * 512
                for i4 in range(4):
                    h = half * 8 + q * 4 + i4
                    cols = slice(h * 128, h * 128 + 128)
                    em("tensor", lambda e, s_=qoff + i4 * 128,
                       cc=cols, st=(i4 == 0), ks=m["ksrc"]: e.matmul(
                           psS[:, s_:s_ + 128], ks[0:64, cc],
                           qT_sb[0:64, cc], start=st, stop=False, **MM),
                       reads=(m["ktk"] + "ab"[half], f"qT{'ab'[half]}"),
                       writes=(f"psS{'ab'[q]}",))
                em("tensor", lambda e, o=qoff, mm_=m["msk"]: e.matmul(
                    psS[:, o:o + 512], idn_sb[:], mm_[:],
                    start=False, stop=True, **MM),
                   reads=("idn", m["mkey"]), writes=(f"psS{'ab'[q]}",))
                em("scalar", lambda e, w_=p_sb[kb][q], o=qoff: e.activation(
                    w_[:], psS[:, o:o + 512], AF.Exp),
                   reads=(f"psS{'ab'[q]}",), writes=(f"p{kb}{q}",))

        def attn_pv(t, half, kb):
            b, j, par, jpar = tbj(t)
            m = kb_meta(t, kb)
            first = (kb == 0) or (j == 0)
            last = (kb == 1)
            for q in range(2):
                qoff = q * 512
                for i4 in range(4):
                    h = half * 8 + q * 4 + i4
                    em("tensor", lambda e, o=qoff + i4 * 128,
                       po=i4 * 128, vv=m["vsrc"], h_=h,
                       st=(first and i4 == 0),
                       sp=(last and i4 == 3), w_=p_sb[kb][q]: e.matmul(
                           psV[0:64, o:o + 128],
                           vv[:, h_ * 64:h_ * 64 + 64],
                           w_[:, po:po + 128],
                           start=st, stop=sp, **MM),
                       reads=(m["vk"], f"p{kb}{q}"),
                       writes=(f"psV{'ab'[q]}",))
                em("tensor", lambda e, o=qoff, w_=p_sb[kb][q],
                   st=first, sp=last: e.matmul(
                       psV[64:128, o:o + 512], onesb[:, 0:64],
                       w_[:, 0:512], start=st, stop=sp, **MM),
                   reads=("onesb", f"p{kb}{q}"), writes=(f"psV{'ab'[q]}",))

        def pv_evac(t, half):
            dst = pvA_sb if half == 0 else pvB_sb
            em("vector", lambda e, d=dst: e.tensor_copy(d[:], psV[0:64, :]),
               reads=("psVa", "psVb"), writes=(f"pv{half}",))

        def r_chain(t, half):
            em("scalar", lambda e: e.activation(
                lnz_sb[:], psV[64:65, :], AF.Ln),
               reads=("psVa", "psVb"), writes=("lnz",))
            em("scalar", lambda e, hf=half: e.activation(
                r2_sb[hf][:], lnz_sb[:], AF.Exp, scale=n1_sb[0:1, 0:1]),
               reads=("lnz", "n1f"), writes=(f"r{half}",))

        def rb_ao(t, half):
            for nh2 in range(2):
                em("tensor", lambda e, n=nh2, hf=half: e.matmul(
                    psS[0:64, n * 512:(n + 1) * 512],
                    onesb[0:1, 0:64],
                    r2_sb[hf][0:1, n * 512:(n + 1) * 512],
                    start=True, stop=True, **MM),
                   reads=("onesb", f"r{half}"), writes=(f"psS{'ab'[nh2]}",))
            pv_src = (pvA_sb if half == 0 else pvB_sb)[:]
            pv_e = pv_src.rearrange(
                "p (b two t) -> p b two t", b=4, two=2)[:, :, 0, :]
            pv_o = pv_src.rearrange(
                "p (b two t) -> p b two t", b=4, two=2)[:, :, 1, :]
            rb_e = psS[0:64, :].rearrange(
                "p (b two t) -> p b two t", b=4, two=2)[:, :, 0, :]
            rb_o = psS[0:64, :].rearrange(
                "p (b two t) -> p b two t", b=4, two=2)[:, :, 1, :]
            pv_keys = (f"pv{half}", "psSa", "psSb")
            ao_t = ao_sb[0:64, half * 512:(half + 1) * 512] \
                .rearrange("p (b t) -> p b t", b=4)
            ao_b = ao_sb[64:128, half * 512:(half + 1) * 512] \
                .rearrange("p (b t) -> p b t", b=4)
            em("vector", lambda e, o=ao_t, i0=pv_e, i1=rb_e:
               e.tensor_tensor(o, i0, i1, ALU.mult),
               reads=pv_keys, writes=(f"ao{half}",))
            em("vector", lambda e, o=ao_b, i0=pv_o, i1=rb_o:
               e.tensor_tensor(o, i0, i1, ALU.mult),
               reads=pv_keys, writes=(f"ao{half}",))

        def oproj_mms(part, nh2):
            for ci in range(part * 4, part * 4 + 4):
                em("tensor", lambda e, c=ci, n=nh2: e.matmul(
                    psS[:, n * 512:(n + 1) * 512],
                    ao_sb[:, c * 128:(c + 1) * 128],
                    wo_sb[:, c * 1024 + n * 512:c * 1024 + (n + 1) * 512],
                    start=(c == 0), stop=False, **MM),
                   reads=(f"ao{part}", "wo"), writes=(f"psS{'ab'[nh2]}",))

        def bo_mm(nh2):
            em("tensor", lambda e, n=nh2: e.matmul(
                psS[:, n * 512:(n + 1) * 512], onesb[0:1, :],
                bo_sb[0:1, n * 512:(n + 1) * 512],
                start=False, stop=True, **MM),
               reads=("onesb", "bo"), writes=(f"psS{'ab'[nh2]}",))

        def x1_half(t, nh2):
            b, j, par, jpar = tbj(t)
            xp = t % 4
            em("vector", lambda e, p=par, xp_=xp, n=nh2: e.tensor_tensor(
                x1_sb[p][:, n * 512:(n + 1) * 512],
                psS[:, n * 512:(n + 1) * 512],
                x_sb[xp_][:, n * 512:(n + 1) * 512], ALU.add),
               reads=(f"psS{'ab'[nh2]}", f"x{xp}"),
               writes=(f"x1{par}{'ab'[nh2]}",))

        def st_ln2a(t):
            b, j, par, jpar = tbj(t)
            em("vector", lambda e, p=par: e.bn_stats(
                st6[p][:, 0:6], x1_sb[p][:, 0:512]),
               reads=(f"x1{par}a",), writes=(f"st6{par}",))

        def st_ln2b(t):
            b, j, par, jpar = tbj(t)
            em("vector", lambda e, p=par: e.bn_stats(
                st6[p][:, 6:12], x1_sb[p][:, 512:1024]),
               reads=(f"x1{par}b",), writes=(f"st6{par}",))
            em("vector", lambda e, p=par: e.bn_aggr(mv[p][:], st6[p][:]),
               reads=(f"st6{par}",), writes=(f"mv{par}",))
            em("scalar", lambda e, p=par: e.activation(
                lnt[p][:], mv[p][:, 1:2], AF.Ln, bias=eps_sb[:]),
               reads=(f"mv{par}", "eps"), writes=(f"lnt{par}",))
            em("scalar", lambda e, p=par: e.activation(
                rstd[p][:], lnt[p][:], AF.Exp, scale=nh_sb[:]),
               reads=(f"lnt{par}", "nhf"), writes=(f"rstd{par}",))
            em("vector", lambda e, p=par: e.tensor_scalar(
                xn2_sb[:], x1_sb[p][:], mv[p][:, 0:1], rstd[p][:, 0:1],
                ALU.subtract, ALU.mult),
               reads=(f"x1{par}a", f"x1{par}b", f"mv{par}", f"rstd{par}"),
               writes=("xn2",))
            em("vector", lambda e, p=par: e.tensor_scalar_add(
                rp1_sb[p][:], rstd[p][:], 1.0),
               reads=(f"rstd{par}",), writes=(f"rp1{par}",))
            em("vector", lambda e, p=par: e.tensor_tensor(
                mr_sb[p][:], mv[p][:, 0:1], rstd[p][:], ALU.mult),
               reads=(f"mv{par}", f"rstd{par}"), writes=(f"mr{par}",))

        def st_xn2T(t):
            transpose8(ptA, "ptA", xn2_sb, "xn2")
            em("vector", lambda e: e.tensor_copy(xn2T_sb[:], ptA[:]),
               reads=("ptA",), writes=("xn2T",))

        def st_gates(t):
            b, j, par, jpar = tbj(t)
            em("tensor", lambda e, b_=b: e.matmul(
                psV[0:1, 608:608 + E], growT_sb[:, b_:b_ + 1], i8f_sb[:],
                start=True, stop=True, **MM),
               reads=("growT", "i8f"), writes=("psVb",))
            em("vector", lambda e: e.tensor_copy(growb_sb[:],
                                                 psV[0:1, 608:608 + E]),
               reads=("psVb",), writes=("growb",))
            em("tensor", lambda e: e.matmul(
                psV[:, 616:616 + E], onesf[:], growb_sb[:],
                start=True, stop=True, **MM),
               reads=("onesf", "growb"), writes=("psVb",))
            em("vector", lambda e: e.tensor_copy(g_sb[:], psV[:, 616:616 + E]),
               reads=("psVb",), writes=("g",))

        def st_mid(t):
            for ci in range(8):
                em("tensor", lambda e, c=ci: e.matmul(
                    psP[:, 0:512], xn2T_sb[:, c * 128:(c + 1) * 128],
                    aa_sb[:, c * 512:(c + 1) * 512],
                    start=(c == 0), stop=(c == 7), **MM),
                   reads=("xn2T", "aall"), writes=("psPa",))

        def st_midg(t):
            mid3 = psP[:, 0:512].rearrange("p (ee rr) -> p ee rr", ee=8)
            g3 = g_sb[:].unsqueeze(2).broadcast_to([128, 8, 64])
            mg3 = midg_sb[:].rearrange("p (ee rr) -> p ee rr", ee=8)
            em("vector", lambda e, o=mg3, i=mid3, gg=g3: e.tensor_tensor(
                o, i, gg, ALU.mult),
               reads=("psPa", "g"), writes=("midg",))
            transpose8(ptA, "ptA", midg_sb, "midg", nchunk=4)
            em("scalar", lambda e: e.activation(midgT_sb[:], ptA[:, 0:512],
                                                AF.Copy),
               reads=("ptA",), writes=("midgT",))

        def moeB_mms(nh2):
            for c2 in range(4):
                em("tensor", lambda e, c=c2, n=nh2: e.matmul(
                    psP[:, n * 512:(n + 1) * 512],
                    midgT_sb[:, c * 128:(c + 1) * 128],
                    bw_sb[:, c * 1024 + n * 512:c * 1024 + (n + 1) * 512],
                    start=(c == 0), stop=(c == 3), **MM),
                   reads=("midgT", "bw"), writes=(f"psP{'ab'[nh2]}",))

        def st_t1(t):
            b, j, par, jpar = tbj(t)
            em("vector", lambda e, p=par: e.tensor_scalar(
                t1_sb[:], x1_sb[p][:], rp1_sb[p][:], mr_sb[p][:],
                ALU.mult, ALU.subtract),
               reads=(f"x1{par}a", f"x1{par}b", f"rp1{par}", f"mr{par}"),
               writes=("t1",))

        def out_half(t, nh2):
            b, j, par, jpar = tbj(t)
            em("vector", lambda e, p=par, n=nh2: e.tensor_tensor(
                out_sb[p][:, n * 512:(n + 1) * 512],
                psP[:, n * 512:(n + 1) * 512],
                t1_sb[:, n * 512:(n + 1) * 512], ALU.add),
               reads=(f"psP{'ab'[nh2]}", "t1"), writes=(f"out{par}",))

        def st_dma_out(t):
            b, j, par, jpar = tbj(t)
            em("sync", lambda e, p=par, b_=b, j_=j: e.dma_start(
                out=outq[b_, j_ * 128:(j_ + 1) * 128, :], in_=out_sb[p][:]),
               reads=(f"out{par}",), sem=f"o{par}", dma=True)

        for t0 in range(3):
            st_load(t0)
        st_ln1(0); st_xnT(0)
        proj_n(wq_sb, "wq", 0); proj_n(wq_sb, "wq", 1)
        st_ropeq(0)
        proj_n(wk_sb, "wk", 0); proj_n(wk_sb, "wk", 1)
        st_ropek(0)
        st_qT(0, 0); st_qT(0, 1); st_kT(0, 0); st_kT(0, 1)
        gates_v0()
        proj_n(wv_sb, "wv", 0); proj_n(wv_sb, "wv", 1)
        st_vevac(0)
        st_ln1(1); st_xnT(1)
        attn_scores(0, 0, 1)
        attn_pv(0, 0, 1)
        pv_evac(0, 0)
        r_chain(0, 0)
        for t in range(NTT):
            n1 = t + 1 < NTT
            n2 = t + 2 < NTT
            b, j, par, jpar = tbj(t)
            j1 = (t + 1) % n_t
            st_load(t + 3)
            if n2:
                st_ln1(t + 2)
            if j > 0:
                attn_scores(t, 1, 0)
            if n1:
                proj_n(wq_sb, "wq", 0)
            attn_scores(t, 1, 1)
            if n1:
                proj_n(wq_sb, "wq", 1)
                st_ropeq(t + 1)
            if j > 0:
                attn_pv(t, 1, 0)
            if n1:
                proj_n(wk_sb, "wk", 0)
            attn_pv(t, 1, 1)
            if n1:
                proj_n(wk_sb, "wk", 1)
                st_ropek(t + 1)
            r_chain(t, 1)
            if t == 0:
                gates_rest()
            if n1:
                st_qT(t + 1, 0)
            rb_ao(t, 0)
            if n1:
                st_qT(t + 1, 1)
                st_kT(t + 1, 0)
            pv_evac(t, 1)
            rb_ao(t, 1)
            if n1:
                st_kT(t + 1, 1)
                proj_n(wv_sb, "wv", 0)
                proj_n(wv_sb, "wv", 1)
                st_vevac(t + 1)
            oproj_mms(0, 0)
            oproj_mms(0, 1)
            oproj_mms(1, 0)
            bo_mm(0)
            x1_half(t, 0)
            st_ln2a(t)
            oproj_mms(1, 1)
            bo_mm(1)
            x1_half(t, 1)
            if j == 0:
                st_gates(t)
            if n1 and j1 > 0:
                attn_scores(t + 1, 0, 0)
            st_ln2b(t)
            if n1:
                attn_scores(t + 1, 0, 1)
            st_t1(t)
            if n1 and j1 > 0:
                attn_pv(t + 1, 0, 0)
            st_xn2T(t)
            st_mid(t)
            if n1:
                attn_pv(t + 1, 0, 1)
            st_midg(t)
            moeB_mms(0)
            moeB_mms(1)
            out_half(t, 0)
            out_half(t, 1)
            st_dma_out(t)
            if n1:
                pv_evac(t + 1, 0)
                r_chain(t + 1, 0)
            if n2:
                st_xnT(t + 2)

        global _LAST_S
        _LAST_S = S
        if max_ops is not None:
            S.ops = S.ops[:max_ops]
        make_body = S.materialize(block, sems, None)
        finals = [(nm, S.totals.get(nm, 0)) for nm in ("o0", "o1")]
        block.sync(make_body("sync"))
        block.gpsimd(make_body("gpsimd", final_waits=finals))
        block.tensor(make_body("tensor"))
        block.vector(make_body("vector"))
        block.scalar(make_body("scalar"))
    return nc


_consts = _host_consts()


def _in_map(x_shard, wq, wk, wv, wo, bo, wr, aall, bwall):
    m = {
        "xq": np.ascontiguousarray(x_shard),
        "wqh": np.ascontiguousarray(wq.T),
        "wkh": np.ascontiguousarray(wk.T),
        "wvh": np.ascontiguousarray(wv.T),
        "woh": np.ascontiguousarray(wo.T),
        "boh": np.ascontiguousarray(bo.reshape(1, D)),
        "wrh": np.ascontiguousarray(wr.T),
        "aah": np.ascontiguousarray(aall),
        "bwh": np.ascontiguousarray(bwall),
        "cqh": _consts["cq"], "sqh": _consts["sq"],
        "ckh": _consts["ck"], "skh": _consts["sk"],
        "idnh": _consts["idn"], "mAh": _consts["maskA"],
        "mBh": _consts["maskB"],
        "i4fh": _consts["i4f"], "i8fh": _consts["i8f"],
    }
    return m


def kernel(L0_attn_query, L0_attn_Wq, L0_attn_Wk, L0_attn_Wv, L0_attn_Wo,
           L0_attn_bo, L0_ffn_Wr, L0_ffn_A, L0_ffn_Bw):
    x = np.asarray(L0_attn_query, dtype=np.float32)
    wq = np.asarray(L0_attn_Wq, np.float32)
    wk = np.asarray(L0_attn_Wk, np.float32)
    wv = np.asarray(L0_attn_Wv, np.float32)
    wo = np.asarray(L0_attn_Wo, np.float32)
    bo = np.asarray(L0_attn_bo, np.float32)
    wr = np.asarray(L0_ffn_Wr, np.float32)
    aall = np.ascontiguousarray(
        np.transpose(np.asarray(L0_ffn_A, np.float32), (2, 0, 1)).reshape(D, ER))
    bwall = np.ascontiguousarray(
        np.transpose(np.asarray(L0_ffn_Bw, np.float32), (0, 2, 1)).reshape(ER, D))

    nc = build_program(BPC, NT)
    in_maps = [_in_map(x[c * BPC:(c + 1) * BPC], wq, wk, wv, wo, bo, wr,
                       aall, bwall) for c in range(NCORE)]
    res = run_bass_kernel_spmd(nc, in_maps, core_ids=list(range(NCORE)))
    return np.concatenate([r["outq"] for r in res.results], axis=0)
